# version 15
# baseline (speedup 1.0000x reference)
import numpy as np
import ml_dtypes
import concourse.bass as bass
import concourse.mybir as mybir
from concourse.bass_utils import run_bass_kernel_spmd

F32 = mybir.dt.float32
BF16 = mybir.dt.bfloat16
AF = mybir.ActivationFunctionType
ALU = mybir.AluOpType
AX = mybir.AxisListType

C = 1024
CG = 8
CH = 128
H = 4096
NE = 8
NQ = 4
HQ = H // NQ

PK_TDS, PK_TMK, PK_TMV, PK_TMR = 0, 8, 16, 24
PK_G, PK_B, PK_BR, PK_IOTA = 32, 40, 48, 56
PK_MASK, PK_ONES, PK_B1 = 64, 65, 193
PK_EPS5 = 193 + NE * 32
PK_EPS8 = PK_EPS5 + 1
PK_ONE = PK_EPS5 + 2
PK_W = PK_EPS5 + 3


class Emit:
    ENGS = ("t", "v", "s", "g", "y")

    def __init__(self, nc):
        self.nc = nc
        self.q = {e: [] for e in self.ENGS}
        self.sem = {}
        self.cnt = {}

    def new_sem(self, name, stack):
        self.sem[name] = stack.enter_context(self.nc.semaphore(name))
        self.cnt[name] = 0

    def op(self, eng, fn, inc=None, k=1):
        if inc is None:
            self.q[eng].append(fn)
            return None
        self.q[eng].append(fn)
        return self.inc_last(eng, inc, k)

    def inc_last(self, eng, sem, k=1):
        self.cnt[sem] += k
        val = self.cnt[sem]
        h = self.sem[sem]
        f = self.q[eng].pop()
        self.q[eng].append(lambda E, f=f, h=h, k=k: f(E).then_inc(h, k))
        return val

    def wait(self, eng, sem, val):
        if val is None or val <= 0:
            return
        h = self.sem[sem]
        self.q[eng].append(lambda E, h=h, v=val: E.wait_ge(h, v))

    def play(self, block):
        def run(E, lst):
            for f in lst:
                f(E)
        block.tensor(lambda E: run(E, self.q["t"]))
        block.vector(lambda E: run(E, self.q["v"]))
        block.scalar(lambda E: run(E, self.q["s"]))
        block.gpsimd(lambda E: run(E, self.q["g"]))
        block.sync(lambda E: run(E, self.q["y"]))

    def clear(self):
        self.q = {e: [] for e in self.ENGS}


def g3(t, ginner):
    return t[:, :].rearrange("p (g c) -> p g c", c=ginner)


def build_program(n_warm=2, n_real=16, dbg=False):
    NCH = n_warm + n_real
    ROWS = NCH * CH
    T = n_real * CH
    NT = n_real
    TCK = 512 if T % 512 == 0 else T
    NTC = T // TCK
    NML = TCK // 128

    nc = bass.Bass("TRN2", target_bir_lowering=False)

    x_in = nc.dram_tensor("x", [ROWS, C], F32, kind="ExternalInput").ap()
    wk_in = nc.dram_tensor("wk", [C, C], F32, kind="ExternalInput").ap()
    wv_in = nc.dram_tensor("wv", [C, C], F32, kind="ExternalInput").ap()
    wr_in = nc.dram_tensor("wr", [C, C], F32, kind="ExternalInput").ap()
    wo_in = nc.dram_tensor("wo", [C, C], F32, kind="ExternalInput").ap()
    w1_in = nc.dram_tensor("w1", [NE, C, H], F32, kind="ExternalInput").ap()
    w2_in = nc.dram_tensor("w2", [NE, H, C], F32, kind="ExternalInput").ap()
    b2_in = nc.dram_tensor("b2", [NE, C], F32, kind="ExternalInput").ap()
    wrf_in = nc.dram_tensor("wrf", [CG, 128, NE], F32, kind="ExternalInput").ap()
    pk_in = nc.dram_tensor("pk", [128, PK_W], F32, kind="ExternalInput").ap()
    eye_in = nc.dram_tensor("eye", [128, 128], F32, kind="ExternalInput").ap()
    y_out = nc.dram_tensor("y", [T, C], F32, kind="ExternalOutput").ap()
    wscr = nc.dram_tensor("wscr", [4, C, C], BF16, kind="Internal").ap()
    dbg_out = {}
    if dbg:
        for nm, shape, dt_ in (
                ("d_xnT", [128, CG * T], BF16), ("d_xlnT", [128, CG * (CH + 1)], BF16),
                ("d_mixk", [128, CG * CH], BF16), ("d_E", [128, CG * CH], F32),
                ("d_V", [128, CG * CH], F32), ("d_stD", [128, CG], F32),
                ("d_stN", [128, CG], F32), ("d_rw", [128, 16 * NE], F32),
                ("d_G", [128, CG * CH], BF16), ("d_WkT", [128, CG * C], BF16),
                ("d_stats", [128, 64], F32), ("d_xlnat", [128, C], BF16),
                ("d_tdb", [128, CG * CH], F32), ("d_wrfb", [128, CG * NE], BF16),
                ("d_tm1m", [128, 32], F32), ("d_sc", [128, 4 * NE], F32)):
            dbg_out[nm] = nc.dram_tensor(nm, shape, dt_, kind="ExternalOutput").ap()

    import contextlib
    stack = contextlib.ExitStack()
    with stack:
        em = Emit(nc)
        for s in ("sx", "sxb", "sc", "sw", "sv", "ss", "st", "so", "sg"):
            em.new_sem(s, stack)

        A = nc.alloc_sbuf_tensor
        acc = A("acc", [128, NT * C], F32)
        xnT = A("xnT", [128, CG * T], BF16)
        pk = A("pkc", [128, PK_W], F32)
        eyeb = A("eyeb", [128, 128], BF16)
        tdb = A("tdb", [128, CG * CH], F32)
        rw_all = A("rw_all", [128, NT * NE], F32)
        rwT = A("rwT", [8, NT * 128], BF16)
        b2bf = A("b2bf", [8, C], BF16)
        wrfb = A("wrfb", [128, CG * NE], BF16)
        scr8 = A("scr8", [128, 256], F32)
        tm1m = A("tm1m", [128, 32], F32)
        trsh = A("trsh", [128, 8], F32)
        rwb = A("rwb", [128, 4 * NE], BF16)
        sc_f = A("sc_f", [128, 4 * NE], F32)

        tds_c = lambda g: pk[:, PK_TDS + g:PK_TDS + g + 1]
        tm_c = {"k": (lambda g: pk[:, PK_TMK + g:PK_TMK + g + 1]),
                "v": (lambda g: pk[:, PK_TMV + g:PK_TMV + g + 1]),
                "r": (lambda g: pk[:, PK_TMR + g:PK_TMR + g + 1])}
        g_c = lambda g: pk[:, PK_G + g:PK_G + g + 1]
        b_c = lambda g: pk[:, PK_B + g:PK_B + g + 1]
        brt = pk[:, PK_BR:PK_BR + NE]
        iota8 = pk[:, PK_IOTA:PK_IOTA + NE]
        mask_c = pk[:, PK_MASK:PK_MASK + 1]
        ones128 = pk[:, PK_ONES:PK_ONES + 128]
        b1_c = lambda e, ht32: pk[:, PK_B1 + e * 32 + ht32:PK_B1 + e * 32 + ht32 + 1]
        tm_t = {"k": pk[:, PK_TMK:PK_TMK + 8], "v": pk[:, PK_TMV:PK_TMV + 8],
                "r": pk[:, PK_TMR:PK_TMR + 8]}
        PIX = {"k": 0, "v": 1, "r": 2}

        def bump(eng, sem):
            if eng == "s":
                return em.op("s", lambda E: E.activation(
                    trsh[:, 0:1], pk[:, 0:1], AF.Copy), inc=sem)
            return em.op("v", lambda E: E.tensor_copy(
                trsh[:, 4:5], pk[:, 0:1]), inc=sem)
        eps5 = pk[:, PK_EPS5:PK_EPS5 + 1]
        eps8 = pk[:, PK_EPS8:PK_EPS8 + 1]
        onec = pk[:, PK_ONE:PK_ONE + 1]

        import contextlib as _ctxlib
        attn_stack = _ctxlib.ExitStack()
        AE = lambda nm, shp, dt_: attn_stack.enter_context(
            nc.sbuf_tensor(nm, shp, dt_))
        WT = {}
        for w in ("k", "v", "r", "o"):
            WT[w] = AE(f"W{w}T", [128, CG * C], BF16)
        xlnTb = [AE(f"xlnT{i}", [128, CG * (CH + 1)], BF16) for i in range(3)]
        mix = {"k": AE("mixk", [128, CG * CH], BF16),
               "v": AE("mixv", [128, CG * CH], BF16)}
        mix["r"] = mix["k"]
        Gbuf = AE("Gbuf", [128, CG * CH], BF16)
        xlnat = AE("xlnat", [128, C], BF16)
        xnat = AE("xnat", [128, C], BF16)
        E2 = [AE(f"Ebuf{i}", [128, CG * CH], F32) for i in range(2)]
        V_s = AE("Vbuf", [128, CG * CH], F32)
        dead = AE("dead", [128, 512], F32)
        stats = AE("stats", [128, 64], F32)
        stD = AE("stD", [128, CG], F32)
        stN = AE("stN", [128, CG], F32)

        pm = [stack.enter_context(nc.psum_tensor(f"pm{i}", [128, 512], F32))
              for i in range(8)]
        ptrA = pm[0][:, :].bitcast(BF16)
        ptrB = pm[1][:, :].bitcast(BF16)
        pbf7 = pm[7][:, :].bitcast(BF16)

        G_ = Gbuf

        em.op("y", lambda E: E.dma_start(out=pk[:, :], in_=pk_in), inc="sc", k=16)
        sx_x = {}
        msF = {}

        em.op("g", lambda E: E.dma_start(out=eyeb[:, :], in_=eye_in), inc="sw", k=16)
        em.op("g", lambda E: E.dma_start(
            out=g3(wrfb, NE)[:, :, :],
            in_=wrf_in.rearrange("g p e -> p g e")), inc="sw", k=16)
        em.op("g", lambda E: E.dma_start(out=b2bf[:, :], in_=b2_in), inc="sw", k=16)
        sw_base = em.cnt["sw"]

        win = {"k": wk_in, "v": wv_in, "r": wr_in, "o": wo_in}
        ms_wcast = {}
        ms_wts_done = {}
        for wi, wname in enumerate(("k", "v", "r", "o")):
            ms_wcast[wname] = em.op(
                "g", lambda E, wi=wi, wname=wname: E.dma_start(
                    out=wscr[wi], in_=win[wname]), inc="sw", k=16)
        for wi, wname in enumerate(("k", "v", "r", "o")):
            em.wait("s", "sw", ms_wcast[wname])
            for cg in range(CG):
                em.op("s", lambda E, wi=wi, wname=wname, cg=cg:
                      E.dma_start(out=g3(WT[wname], C)[:, cg, :],
                                  in_=wscr[wi][:, cg * 128:(cg + 1) * 128],
                                  transpose=True), inc="sw", k=16)
            ms_wts_done[wname] = em.cnt["sw"]

        em.wait("s", "sc", 16)
        em.op("s", lambda E: E.activation(
            tm1m[:, 24:32], pk[:, PK_TDS:PK_TDS + 8], AF.Exp))
        ms_etd = bump("s", "ss")
        em.wait("v", "sc", 16)
        em.wait("v", "ss", ms_etd)
        for g in range(CG):
            em.op("v", lambda E, g=g: E.tensor_scalar(
                tdb[:, g * CH:(g + 1) * CH], ones128[:, 0:CH],
                tm1m[:, 24 + g:24 + g + 1], None, ALU.mult))
        em.op("v", lambda E: E.tensor_scalar(
            stD[:, :], iota8, 0.0, None, ALU.mult))
        em.op("v", lambda E: E.tensor_scalar(
            stN[:, :], iota8, 0.0, None, ALU.mult))
        for pi, p in enumerate(("k", "v", "r")):
            em.op("v", lambda E, pi=pi, p=p: E.tensor_scalar(
                tm1m[:, pi * 8:pi * 8 + 8], tm_t[p], -1.0, 1.0,
                ALU.mult, ALU.add))

        STAT0 = lambda par, i: stats[:, (par % 4) * 16 + i:(par % 4) * 16 + i + 1]
        msA = {}; msG = {}; msH = {}; msI = {}
        msJk = {}; msJv = {}; msK = {}; msL = {}; msM = {}; msN = {}; msHs = {}
        msO = {}; msP = {}; msQ = {}; msR = {}; msV = {}
        bankfree23 = {}
        msXD = {}

        def emitA(c):
            real = c >= n_warm
            m = c - n_warm
            xin = acc[:, (m if real else n_real - n_warm + c) * C:
                      (m if real else n_real - n_warm + c) * C + C]
            STAT = lambda i: STAT0(c % 4, i)

            if real and c - n_real >= 0:
                em.wait("y", "sv", msF[c - n_real])
            xsem = "sx" if c % 2 == 0 else "sxb"
            em.op("y", lambda E, xr=x_in[c * CH:(c + 1) * CH, :], xd=xin:
                  E.dma_start(out=xd, in_=xr), inc=xsem, k=16)
            msXD[c] = (xsem, em.cnt[xsem])

            em.wait("s", msXD[c][0], msXD[c][1])
            if c >= 4:
                em.wait("s", "sv", msF[c - 4])
            if c >= 1:
                em.wait("s", "st", msG[c - 1])
            em.op("s", lambda E, xd=xin: E.activation(
                xnat[:, :], xd, AF.Copy, accum_out=STAT(0)))
            em.op("s", lambda E, xd=xin: E.activation(
                xnat[:, :], xd, AF.Square, accum_out=STAT(1)))
            em.op("s", lambda E: E.mul(STAT(2), STAT(0), -1.0 / C))
            em.op("s", lambda E: E.activation(STAT(4), STAT(2), AF.Square))
            em.op("s", lambda E: E.mul(STAT(5), STAT(4), -1.0))
            em.op("s", lambda E: E.mul(STAT(3), STAT(1), 1.0 / C))
            em.op("s", lambda E: E.add(STAT(6), STAT(3), STAT(5)))
            em.op("s", lambda E: E.activation(STAT(7), STAT(6), AF.Ln, bias=eps5))
            em.op("s", lambda E: E.activation(
                STAT(8), STAT(7), AF.Exp, scale=-0.5))
            em.op("s", lambda E: E.activation(STAT(10), STAT(6), AF.Ln))
            em.op("s", lambda E: E.mul(STAT(9), STAT(7), -1.0))
            em.op("s", lambda E: E.mul(STAT(14), STAT(7), -0.5))
            em.op("s", lambda E: E.activation(STAT(12), STAT(10), AF.Exp,
                                              bias=STAT(9)))
            em.op("s", lambda E: E.activation(STAT(13), STAT(12), AF.Ln, bias=eps5))
            em.op("s", lambda E: E.activation(
                STAT(15), STAT(13), AF.Exp, scale=-0.5, bias=STAT(14)))
            msA[c] = bump("s", "ss")

            em.wait("v", "ss", msA[c])
            if c >= 1:
                em.wait("v", "st", msG[c - 1])
            fn = lambda E, xd=xin: E.tensor_scalar(
                xlnat[:, :], xd, STAT(2), STAT(15), ALU.add, ALU.mult)
            if real:
                em.op("v", fn)
                msF[c] = em.op("v", lambda E, xd=xin: E.tensor_scalar(
                    xnat[:, :], xd, STAT(2), STAT(8), ALU.add, ALU.mult),
                    inc="sv")
            else:
                msF[c] = em.op("v", fn, inc="sv")

            em.wait("t", "sv", msF[c])
            if c >= 1:
                em.wait("t", "sv", msH[c - 1])
                if (c - 1) in msHs:
                    em.wait("t", "ss", msHs[c - 1])
            if real:
                for cg in range(CG):
                    em.op("t", lambda E, cg=cg: E.transpose(
                        ptrA[:, cg * 128:(cg + 1) * 128],
                        xnat[:, cg * 128:(cg + 1) * 128], eyeb[:, :]))
            for cg in range(CG):
                fn = lambda E, cg=cg: E.transpose(
                    ptrB[:, cg * 128:(cg + 1) * 128],
                    xlnat[:, cg * 128:(cg + 1) * 128], eyeb[:, :])
                if cg == CG - 1:
                    msG[c] = em.op("t", fn, inc="st")
                else:
                    em.op("t", fn)

            em.wait("v", "st", msG[c])
            xlnT3 = g3(xlnTb[c % 3], CH + 1)
            if c == 0:
                em.op("v", lambda E: E.tensor_scalar(
                    xlnT3[:, :, 0], iota8[:, :], 0.0, None, ALU.mult))
            else:
                prev3 = g3(xlnTb[(c - 1) % 3], CH + 1)
                em.op("v", lambda E, p3=prev3: E.tensor_copy(
                    xlnT3[:, :, 0], p3[:, :, CH]))
            if real:
                em.wait("s", "st", msG[c])
                em.op("s", lambda E, m=m: E.copy(
                    g3(xnT, T)[:, :, m * CH:(m + 1) * CH],
                    ptrA.rearrange("p (g c) -> p g c", c=128)))
                msHs[c] = em.inc_last("s", "ss")
            for cg in range(CG):
                fn = lambda E, cg=cg: E.tensor_scalar(
                    xlnT3[:, cg, 1:CH + 1], ptrB[:, cg * 128:(cg + 1) * 128],
                    g_c(cg), b_c(cg), ALU.mult, ALU.add)
                if cg == CG - 1:
                    msH[c] = em.op("v", fn, inc="sv")
                else:
                    em.op("v", fn)

        def emitB(c):
            real = c >= n_warm
            m = c - n_warm
            E_ = E2[c % 2]
            V_ = V_s
            xlnT3 = g3(xlnTb[c % 3], CH + 1)

            def emit_mix(p):
                for cg in range(CG):
                    em.op("v", lambda E, p=p, cg=cg: E.tensor_scalar(
                        mix[p][:, cg * CH:(cg + 1) * CH],
                        xlnT3[:, cg, 1:CH + 1], tm_c[p](cg), None, ALU.mult))
                for cg in range(CG):
                    em.op("v", lambda E, p=p, cg=cg: E.scalar_tensor_tensor(
                        mix[p][:, cg * CH:(cg + 1) * CH],
                        xlnT3[:, cg, 0:CH],
                        tm1m[:, PIX[p] * 8 + cg:PIX[p] * 8 + cg + 1],
                        mix[p][:, cg * CH:(cg + 1) * CH],
                        ALU.mult, ALU.add))

            def emit_patch(pi, p):
                em.op("v", lambda E, p=p, pi=pi: E.tensor_mul(
                    scr8[:, 16 + 8 * pi:24 + 8 * pi], scr8[:, 8:16], tm_t[p]))
                em.op("v", lambda E: E.tensor_copy(trsh[:, 4:5], pk[:, 0:1]))
                em.op("v", lambda E, p=p, pi=pi: E.tensor_add(
                    g3(mix[p], CH)[:, :, 0], scr8[:, 16 + 8 * pi:24 + 8 * pi],
                    scr8[:, 0:8]))

            em.wait("v", "st", msG[c])
            if c >= 1:
                if (c - 1) in msL:
                    em.wait("v", "st", msL[c - 1])
                em.wait("v", "st", msJv[c - 1])
            if c == n_warm:
                em.op("v", lambda E: E.tensor_scalar(
                    scr8[:, 0:8], xlnT3[:, :, 0], mask_c, None, ALU.mult))
                em.op("v", lambda E: E.tensor_copy(trsh[:, 4:5], pk[:, 0:1]))
                em.op("v", lambda E: E.tensor_sub(
                    scr8[:, 8:16], xlnT3[:, :, 1], scr8[:, 0:8]))
                em.op("v", lambda E: E.tensor_copy(trsh[:, 4:5], pk[:, 0:1]))
            emit_mix("k")
            emit_mix("v")
            if c == n_warm:
                emit_patch(0, "k")
                emit_patch(1, "v")
            msI[c] = em.inc_last("v", "sv")

            em.wait("t", "sv", msI[c])
            if c >= 1:
                em.wait("t", "ss", bankfree23[c - 1])
                em.wait("t", "ss", msV[c - 1])
            if (c - 2) in msQ:
                em.wait("t", "sv", msQ[c - 2])
            if c == 0:
                em.wait("t", "sw", ms_wts_done["v"])
            for (p, bofs, msd) in (("k", 2, msJk), ("v", 4, msJv)):
                for coT in range(CG):
                    bank = pm[bofs + coT // 4]
                    cofs = (coT % 4) * 128
                    for cg in range(CG):
                        fn = (lambda E, p=p, coT=coT, cg=cg, b=bank, co=cofs:
                              E.matmul(b[:, co:co + 128],
                                       g3(WT[p], C)[:, cg, coT * 128:(coT + 1) * 128],
                                       g3(mix[p], CH)[:, cg, :],
                                       start=(cg == 0), stop=(cg == CG - 1)))
                        if coT == CG - 1 and cg == CG - 1:
                            msd[c] = em.op("t", fn, inc="st")
                        else:
                            em.op("t", fn)

            em.wait("s", "st", msJk[c])
            if c >= 2:
                em.wait("s", "sg", msP[c - 2])
            em.op("s", lambda E: E.activation(E_[:, 0:512], pm[2][:, :], AF.Exp))
            em.op("s", lambda E: E.activation(E_[:, 512:1024], pm[3][:, :], AF.Exp))
            em.wait("s", "st", msJv[c])
            if c >= 1:
                em.wait("s", "sg", msP[c - 1])
            em.op("s", lambda E: E.copy(V_[:, 0:512], pm[4][:, :]))
            msK[c] = em.op("s", lambda E: E.copy(
                V_[:, 512:1024], pm[5][:, :]), inc="ss")
            msV[c] = msK[c]
            bankfree23[c] = msK[c]
            emitWo(c - 1, msK[c])

            if real:
                em.wait("v", "st", msJk[c])
                emit_mix("r")
                if c == n_warm:
                    emit_patch(2, "r")
                msIr = em.inc_last("v", "sv")

            if real:
                em.wait("t", "sv", msIr)
                if (c - 1) in msO:
                    em.wait("t", "ss", msO[c - 1])
                if c == n_warm:
                    em.wait("t", "sw", ms_wts_done["r"])
                for coT in range(CG):
                    bank = pm[6 + coT // 4]
                    cofs = (coT % 4) * 128
                    for cg in range(CG):
                        fn = (lambda E, coT=coT, cg=cg, b=bank, co=cofs:
                              E.matmul(b[:, co:co + 128],
                                       g3(WT["r"], C)[:, cg, coT * 128:(coT + 1) * 128],
                                       g3(mix["r"], CH)[:, cg, :],
                                       start=(cg == 0), stop=(cg == CG - 1)))
                        if coT == CG - 1 and cg == CG - 1:
                            msL[c] = em.op("t", fn, inc="st")
                        else:
                            em.op("t", fn)

            em.wait("g", "ss", msK[c])
            msKV = em.op("g", lambda E: E.tensor_mul(
                V_[:, :], E_[:, :], V_[:, :]), inc="sg")
            em.wait("v", "ss", msK[c])
            em.wait("v", "sg", msKV)
            for cg in range(CG):
                sl = slice(cg * CH, (cg + 1) * CH)
                em.op("v", lambda E, sl=sl, cg=cg: E.tensor_tensor_scan(
                    E_[:, sl], tdb[:, sl], E_[:, sl], stD[:, cg:cg + 1],
                    ALU.mult, ALU.add))
            for cg in range(CG):
                sl = slice(cg * CH, (cg + 1) * CH)
                em.op("v", lambda E, sl=sl, cg=cg: E.tensor_tensor_scan(
                    V_[:, sl], tdb[:, sl], V_[:, sl], stN[:, cg:cg + 1],
                    ALU.mult, ALU.add))
            em.op("v", lambda E: E.tensor_copy(stD[:, :], g3(E_, CH)[:, :, CH - 1]))
            em.op("v", lambda E: E.tensor_copy(trsh[:, 4:5], pk[:, 0:1]))
            msM[c] = em.op("v", lambda E: E.tensor_copy(
                stN[:, :], g3(V_, CH)[:, :, CH - 1]), inc="sv")
            emitAcc(c - 1)

            em.wait("s", "sv", msM[c])
            for hh in range(2):
                em.op("s", lambda E, hh=hh: E.activation(
                    dead[:, :], E_[:, hh * 512:(hh + 1) * 512], AF.Ln,
                    bias=eps8))
                fn = lambda E, hh=hh: E.activation(
                    E_[:, hh * 512:(hh + 1) * 512], dead[:, :], AF.Exp,
                    scale=-1.0)
                if hh == 1:
                    msN[c] = em.op("s", fn, inc="ss")
                else:
                    em.op("s", fn)

            if real:
                em.wait("s", "st", msL[c])
                if (c - 1) in msR:
                    em.wait("s", "st", msR[c - 1])
                for bi, b in enumerate((6, 7)):
                    em.op("s", lambda E, b=b: E.activation(
                        dead[:, 0:512], pm[b][:, :], AF.Exp, scale=-1.0))
                    em.op("s", lambda E: E.activation(
                        dead[:, 0:512], dead[:, 0:512], AF.Ln, bias=onec))
                    fn = lambda E, bi=bi: E.activation(
                        G_[:, bi * 512:(bi + 1) * 512], dead[:, 0:512],
                        AF.Exp, scale=-1.0)
                    if bi == 1:
                        msO[c] = em.op("s", fn, inc="ss")
                    else:
                        em.op("s", fn)

            em.wait("g", "ss", msN[c])
            em.wait("g", "sv", msM[c])
            fn = lambda E: E.tensor_mul(V_[:, :], V_[:, :], E_[:, :])
            if real:
                em.op("g", fn)
                em.wait("g", "ss", msO[c])
                msP[c] = em.op("g", lambda E: E.tensor_mul(
                    G_[:, :], V_[:, :], G_[:, :]), inc="sg")
            else:
                msP[c] = em.op("g", fn, inc="sg")



        def emitWo(c, free_ss):
            if c < n_warm:
                return
            m = c - n_warm
            em.wait("t", "sg", msP[c])
            em.wait("t", "ss", free_ss)
            if (c - 1) in msQ:
                em.wait("t", "sv", msQ[c - 1])
            if c == n_warm:
                em.wait("t", "sw", ms_wts_done["o"])
            for half in range(2):
                for cg in range(CG):
                    fn = (lambda E, half=half, cg=cg:
                          E.matmul(pm[2 + half][:, :],
                                   g3(G_, CH)[:, cg, :],
                                   g3(WT["o"], C)[:, cg,
                                                  half * 512:(half + 1) * 512],
                                   start=(cg == 0), stop=(cg == CG - 1)))
                    if half == 1 and cg == CG - 1:
                        msR[c] = em.op("t", fn, inc="st")
                    else:
                        em.op("t", fn)

        def emitAcc(c):
            if c < n_warm:
                return
            m = c - n_warm
            em.wait("v", "st", msR[c])
            em.op("v", lambda E, m=m: E.tensor_add(
                acc[:, m * C:m * C + 512], acc[:, m * C:m * C + 512],
                pm[2][:, :]))
            msQ[c] = em.op("v", lambda E, m=m: E.tensor_add(
                acc[:, m * C + 512:m * C + 1024],
                acc[:, m * C + 512:m * C + 1024], pm[3][:, :]), inc="sv")

        emitA(0)
        emitA(1)
        for c in range(NCH):
            if c + 2 < NCH:
                emitA(c + 2)
            emitB(c)
        emitWo(NCH - 1, msK[NCH - 1])
        emitAcc(NCH - 1)

        last = NCH - 1
        attn_v_done = em.cnt["sv"]
        attn_s_done = em.cnt["ss"]

        ms_b2 = {}

        GL = 4 if NT % 4 == 0 else (2 if NT % 2 == 0 else 1)
        NG = NT // GL
        ms_grp_read = {}
        em.wait("t", "sv", msQ[last])
        em.wait("t", "ss", attn_s_done)
        LC = lambda l, a, b: scr8[:, l * 64 + a:l * 64 + b]
        SCL = lambda l: sc_f[:, l * NE:(l + 1) * NE]
        for grp in range(NG):
            mis = [grp * GL + l for l in range(GL)]
            if grp >= 1:
                em.wait("t", "sv", ms_grp_read[grp - 1])
            for l, mi in enumerate(mis):
                for cg in range(CG):
                    em.op("t", lambda E, l=l, mi=mi, cg=cg: E.matmul(
                        pm[6][:, l * NE:l * NE + NE],
                        g3(xnT, T)[:, cg, mi * CH:mi * CH + 128],
                        g3(wrfb, NE)[:, cg, :],
                        start=(cg == 0), stop=(cg == CG - 1)))
            ms_scg = em.inc_last("t", "st")
            em.wait("v", "st", ms_scg)

            def lanes(fn):
                for l in range(GL):
                    em.op("v", lambda E, l=l: fn(E, l))
            lanes(lambda E, l: E.tensor_add(
                SCL(l), pm[6][:, l * NE:l * NE + NE], brt))
            ms_grp_read[grp] = bump("v", "sv")
            lanes(lambda E, l: E.tensor_scalar(
                SCL(l), SCL(l), 0.0, None, ALU.max))
            lanes(lambda E, l: E.reduce_max(LC(l, 0, 1), SCL(l), axis=AX.X))
            lanes(lambda E, l: E.tensor_scalar(
                LC(l, 8, 16), SCL(l), LC(l, 0, 1), None, ALU.is_equal))
            lanes(lambda E, l: E.tensor_scalar(
                LC(l, 8, 16), LC(l, 8, 16), -1e9, 1e9, ALU.mult, ALU.add))
            lanes(lambda E, l: E.tensor_add(LC(l, 8, 16), LC(l, 8, 16), iota8))
            lanes(lambda E, l: E.tensor_reduce(
                LC(l, 1, 2), LC(l, 8, 16), axis=AX.X, op=ALU.min))
            lanes(lambda E, l: E.tensor_scalar(
                LC(l, 16, 24), iota8, LC(l, 1, 2), None, ALU.is_equal))
            lanes(lambda E, l: E.scalar_tensor_tensor(
                LC(l, 24, 32), LC(l, 16, 24), -1e9, SCL(l), ALU.mult, ALU.add))
            lanes(lambda E, l: E.reduce_max(LC(l, 2, 3), LC(l, 24, 32), axis=AX.X))
            lanes(lambda E, l: E.tensor_scalar(
                LC(l, 32, 40), LC(l, 24, 32), LC(l, 2, 3), None, ALU.is_equal))
            lanes(lambda E, l: E.tensor_scalar(
                LC(l, 32, 40), LC(l, 32, 40), -1e9, 1e9, ALU.mult, ALU.add))
            lanes(lambda E, l: E.tensor_add(LC(l, 32, 40), LC(l, 32, 40), iota8))
            lanes(lambda E, l: E.tensor_reduce(
                LC(l, 3, 4), LC(l, 32, 40), axis=AX.X, op=ALU.min))
            lanes(lambda E, l: E.tensor_scalar(
                LC(l, 40, 48), iota8, LC(l, 3, 4), None, ALU.is_equal))
            lanes(lambda E, l: E.tensor_sub(LC(l, 4, 5), LC(l, 2, 3), LC(l, 0, 1)))
            ms_d = bump("v", "sv")
            em.wait("s", "sv", ms_d)
            for l in range(GL):
                em.op("s", lambda E, l=l: E.activation(
                    LC(l, 5, 6), LC(l, 4, 5), AF.Exp))
            ms_e2 = bump("s", "ss")
            em.wait("v", "ss", ms_e2)
            lanes(lambda E, l: E.tensor_scalar(
                LC(l, 6, 7), LC(l, 5, 6), 1.0, None, ALU.add))
            lanes(lambda E, l: E.reciprocal(LC(l, 6, 7), LC(l, 6, 7)))
            lanes(lambda E, l: E.tensor_mul(LC(l, 7, 8), LC(l, 5, 6), LC(l, 6, 7)))
            lanes(lambda E, l: E.tensor_scalar(
                LC(l, 48, 56), LC(l, 16, 24), LC(l, 6, 7), None, ALU.mult))
            lanes(lambda E, l: E.tensor_scalar(
                LC(l, 56, 64), LC(l, 40, 48), LC(l, 7, 8), None, ALU.mult))
            for l, mi in enumerate(mis):
                em.op("v", lambda E, l=l, mi=mi: E.tensor_add(
                    rw_all[:, mi * NE:(mi + 1) * NE],
                    LC(l, 48, 56), LC(l, 56, 64)))
            for l, mi in enumerate(mis):
                em.op("v", lambda E, l=l, mi=mi: E.tensor_copy(
                    rwb[:, l * NE:(l + 1) * NE],
                    rw_all[:, mi * NE:(mi + 1) * NE]))
            ms_rwbg = bump("v", "sv")
            em.wait("t", "sv", ms_rwbg)
            for l in range(GL):
                em.op("t", lambda E, l=l: E.transpose(
                    pbf7[0:8, l * 128:(l + 1) * 128],
                    rwb[:, l * NE:(l + 1) * NE], eyeb[:, :]))
            ms_rwtg = em.inc_last("t", "st")
            em.wait("v", "st", ms_rwtg)
            for l, mi in enumerate(mis):
                em.op("v", lambda E, l=l, mi=mi: E.tensor_copy(
                    rwT[:, mi * 128:(mi + 1) * 128],
                    pbf7[0:8, l * 128:(l + 1) * 128]))
            ms_cpg = bump("v", "sv")
            em.wait("t", "sv", ms_cpg)
            for l, mi in enumerate(mis):
                for half in range(2):
                    msb = em.op("t", lambda E, mi=mi, half=half: E.matmul(
                        pm[4 + half][:, :], rwT[:, mi * 128:(mi + 1) * 128],
                        b2bf[:, half * 512:(half + 1) * 512],
                        start=True, stop=True), inc="st")
                    em.wait("v", "st", msb)
                    ms_b2[(mi, half)] = em.op(
                        "v", lambda E, mi=mi, half=half: E.tensor_add(
                            acc[:, mi * C + half * 512:mi * C + half * 512 + 512],
                            acc[:, mi * C + half * 512:mi * C + half * 512 + 512],
                            pm[4 + half][:, :]), inc="sv")
                    em.wait("t", "sv", ms_b2[(mi, half)])

        router_v_done = em.cnt["sv"]
        router_s_done = em.cnt["ss"]
        router_t_done = em.cnt["st"]

        with nc.Block() as blk1:
            em.play(blk1)
        em.clear()
        if not dbg:
            attn_stack.close()
        else:
            stack.enter_context(attn_stack)
        w1q = [stack.enter_context(nc.sbuf_tensor(f"w1q{i}", [128, CG * HQ],
                                                  BF16)) for i in range(2)]
        w2q = [stack.enter_context(nc.sbuf_tensor(f"w2q{i}", [128, CG * HQ],
                                                  BF16)) for i in range(2)]
        hT = [stack.enter_context(nc.sbuf_tensor(f"hT{i}", [128, CG * TCK],
                                                 BF16)) for i in range(2)]

        ms_wffn = {}
        ms_ffn_t = {}

        def emit_ffn_dma(i):
            e, q = divmod(i, NQ)
            if i >= 2:
                em.wait("g", "st", ms_ffn_t[i - 2])
            elif i == 0:
                em.wait("g", "st", msL[NCH - 1])
                em.wait("g", "sc", 16)
            else:
                em.wait("g", "st", msR[NCH - 1])
                em.wait("g", "sc", 16)
            src1 = w1_in[e, :, q * HQ:(q + 1) * HQ] \
                .rearrange("(g p) h -> p g h", p=128)
            src2 = w2_in[e, q * HQ:(q + 1) * HQ, :] \
                .rearrange("(g p) d -> p g d", p=128)
            em.op("g", lambda E, s=src1, i=i: E.dma_start(
                out=g3(w1q[i % 2], HQ)[:, :, :], in_=s), inc="sw", k=16)
            ms_wffn[i] = em.op("g", lambda E, s=src2, i=i: E.dma_start(
                out=g3(w2q[i % 2], HQ)[:, :, :], in_=s), inc="sw", k=16)

        emit_ffn_dma(0)
        emit_ffn_dma(1)

        em.wait("t", "sv", router_v_done)
        em.wait("t", "ss", attn_s_done)
        bank_free_h = {}
        bank_free_y = {}
        hT_reader = {}
        h_done = {}
        NJ = NE * NQ * NTC

        def emit_h(k):
            i, n = divmod(k, NTC)
            e, q = divmod(i, NQ)
            if n == 0:
                em.wait("t", "sw", ms_wffn[i])
            for ht in range(CG):
                b = 2 + (ht % 2)
                if b in bank_free_h:
                    em.wait("t", "ss", bank_free_h[b])
                for cg in range(CG):
                    fn = (lambda E, i=i, n=n, ht=ht, cg=cg, b=b:
                          E.matmul(pm[b][:, 0:TCK],
                                   g3(w1q[i % 2], HQ)[:, cg,
                                                      ht * 128:(ht + 1) * 128],
                                   g3(xnT, T)[:, cg, n * TCK:(n + 1) * TCK],
                                   start=(cg == 0), stop=(cg == CG - 1)))
                    em.op("t", fn)
                ms_h = em.inc_last("t", "st")
                em.wait("s", "st", ms_h)
                if ht == 0 and (k % 2) in hT_reader:
                    em.wait("s", "st", hT_reader[k % 2])
                bank_free_h[b] = em.op(
                    "s", lambda E, e=e, q=q, k=k, ht=ht, b=b: E.activation(
                        g3(hT[k % 2], TCK)[:, ht, :], pm[b][:, 0:TCK],
                        AF.Relu, bias=b1_c(e, q * 8 + ht)), inc="ss")
            h_done[k] = em.cnt["ss"]

        def emit_y(k):
            i, n = divmod(k, NTC)
            e, q = divmod(i, NQ)
            em.wait("t", "ss", h_done[k])
            for ml in range(NML):
                mi = n * NML + ml
                for half in range(2):
                    b = 4 + half
                    if b in bank_free_y:
                        em.wait("t", "sv", bank_free_y[b])
                    for hg in range(CG):
                        fn = (lambda E, i=i, k=k, ml=ml, half=half, hg=hg, b=b:
                              E.matmul(pm[b][:, :],
                                       g3(hT[k % 2], TCK)[:, hg,
                                                          ml * 128:(ml + 1) * 128],
                                       g3(w2q[i % 2], HQ)[:, hg,
                                                          half * 512:(half + 1) * 512],
                                       start=(hg == 0), stop=(hg == CG - 1)))
                        em.op("t", fn)
                    ms_y = em.inc_last("t", "st")
                    hT_reader[k % 2] = ms_y
                    em.wait("v", "st", ms_y)
                    bank_free_y[b] = em.op(
                        "v", lambda E, e=e, mi=mi, half=half, b=b:
                        E.scalar_tensor_tensor(
                            acc[:, mi * C + half * 512:mi * C + half * 512 + 512],
                            pm[b][:, :],
                            rw_all[:, mi * NE + e:mi * NE + e + 1],
                            acc[:, mi * C + half * 512:mi * C + half * 512 + 512],
                            ALU.mult, ALU.add), inc="sv")

        emit_h(0)
        for k in range(NJ):
            if k + 1 < NJ:
                emit_h(k + 1)
            emit_y(k)
            i, n = divmod(k, NTC)
            if n == NTC - 1:
                ms_ffn_t[i] = em.cnt["st"]
                if i + 2 < NE * NQ:
                    emit_ffn_dma(i + 2)

        em.wait("y", "sv", em.cnt["sv"])
        em.wait("y", "ss", em.cnt["ss"])
        em.wait("y", "st", em.cnt["st"])
        if dbg:
            ndb = 0
            for nm, buf in (("d_xnT", xnT), ("d_xlnT", xlnTb[(NCH - 1) % 2]), ("d_mixk", mix["k"]),
                            ("d_E", E2[(NCH - 1) % 2]), ("d_V", V_s), ("d_stD", stD),
                            ("d_stN", stN), ("d_G", G_), ("d_WkT", WT["k"]),
                            ("d_stats", stats), ("d_xlnat", xlnat),
                            ("d_tdb", tdb), ("d_wrfb", wrfb),
                            ("d_tm1m", tm1m), ("d_sc", sc_f)):
                em.op("y", lambda E, o=dbg_out[nm], b=buf: E.dma_start(
                    out=o, in_=b[:, :]), inc="so", k=16)
                ndb += 1
            em.op("y", lambda E, o=dbg_out["d_rw"], b=rw_all: E.dma_start(
                out=o[:, 0:NT * NE], in_=b[:, :]), inc="so", k=16)
            ndb += 1
            em.wait("y", "so", 16 * ndb)
        em.op("y", lambda E: E.dma_start(
            out=y_out.rearrange("(m p) d -> p m d", p=128),
            in_=acc[:, :].rearrange("p (m d) -> p m d", d=C)), inc="so", k=16)
        em.wait("y", "so", 16)

        with nc.Block() as blk2:
            em.play(blk2)
        em.clear()

    return nc


def _shard_inputs(inputs, n_warm=2, n_real=16):
    x = np.ascontiguousarray(inputs["L0_attn_x"], dtype=np.float32)
    B, S, _ = x.shape
    T = n_real * CH
    PRE = n_warm * CH
    ncore_per_b = S // T

    def pack_row(a):
        return np.ascontiguousarray(a, dtype=np.float32).reshape(CG, 128).T

    pk = np.zeros((128, PK_W), np.float32)
    pk[:, PK_TDS:PK_TDS + 8] = pack_row(inputs["L0_attn_time_decay"])
    pk[:, PK_TMK:PK_TMK + 8] = pack_row(inputs["L0_attn_time_mix_k"].reshape(-1))
    pk[:, PK_TMV:PK_TMV + 8] = pack_row(inputs["L0_attn_time_mix_v"].reshape(-1))
    pk[:, PK_TMR:PK_TMR + 8] = pack_row(inputs["L0_attn_time_mix_r"].reshape(-1))
    pk[:, PK_G:PK_G + 8] = pack_row(inputs["L0_attn_ln_g"])
    pk[:, PK_B:PK_B + 8] = pack_row(inputs["L0_attn_ln_b"])
    pk[:, PK_BR:PK_BR + 8] = np.broadcast_to(
        np.asarray(inputs["L0_ffn_br"], np.float32), (128, 8))
    pk[:, PK_IOTA:PK_IOTA + 8] = np.broadcast_to(
        np.arange(8, dtype=np.float32), (128, 8))
    pk[:, PK_ONES:PK_ONES + 128] = 1.0
    b1 = np.asarray(inputs["L0_ffn_b1"], np.float32)
    pk[:, PK_B1:PK_B1 + 256] = b1.reshape(8, 32, 128).transpose(2, 0, 1) \
        .reshape(128, 256)
    pk[:, PK_EPS5] = 1e-5
    pk[:, PK_EPS8] = 1e-8
    pk[:, PK_ONE] = 1.0

    common = {
        "wk": np.ascontiguousarray(inputs["L0_attn_Wk"], np.float32),
        "wv": np.ascontiguousarray(inputs["L0_attn_Wv"], np.float32),
        "wr": np.ascontiguousarray(inputs["L0_attn_Wr"], np.float32),
        "wo": np.ascontiguousarray(inputs["L0_attn_Wo"], np.float32),
        "w1": np.ascontiguousarray(inputs["L0_ffn_W1"], np.float32),
        "w2": np.ascontiguousarray(inputs["L0_ffn_W2"], np.float32),
        "b2": np.ascontiguousarray(inputs["L0_ffn_b2"], np.float32),
        "wrf": np.ascontiguousarray(inputs["L0_ffn_Wr"], np.float32)
            .reshape(CG, 128, NE),
        "eye": np.eye(128, dtype=np.float32),
    }

    in_maps = []
    for b in range(B):
        for j in range(ncore_per_b):
            t0 = j * T
            if t0 - PRE >= 0:
                xs = x[b, t0 - PRE:t0 + T]
                msk = 1.0
            else:
                xs = np.concatenate(
                    [np.zeros((PRE - t0, C), np.float32), x[b, 0:t0 + T]], axis=0)
                msk = 0.0
            pkc = pk.copy()
            pkc[:, PK_MASK] = msk
            m = dict(common)
            m["x"] = np.ascontiguousarray(xs)
            m["pk"] = pkc
            in_maps.append(m)
    return in_maps


_NC_CACHE = {}


def kernel(**inputs):
    n_warm, n_real = 2, 16
    key = (n_warm, n_real)
    if key not in _NC_CACHE:
        _NC_CACHE[key] = build_program(n_warm, n_real)
    nc = _NC_CACHE[key]
    in_maps = _shard_inputs(inputs, n_warm, n_real)
    res = run_bass_kernel_spmd(nc, in_maps, core_ids=list(range(8)))
    x = inputs["L0_attn_x"]
    B, S, _ = x.shape
    T = n_real * CH
    out = np.empty((B, S, C), np.float32)
    ci = 0
    for b in range(B):
        for j in range(S // T):
            out[b, j * T:(j + 1) * T] = res.results[ci]["y"]
            ci += 1
    return out


# version 16
# speedup vs baseline: 1.0004x; 1.0004x over previous
import numpy as np
import ml_dtypes
import concourse.bass as bass
import concourse.mybir as mybir
from concourse.bass_utils import run_bass_kernel_spmd

F32 = mybir.dt.float32
BF16 = mybir.dt.bfloat16
AF = mybir.ActivationFunctionType
ALU = mybir.AluOpType
AX = mybir.AxisListType

C = 1024
CG = 8
CH = 128
H = 4096
NE = 8
NQ = 4
HQ = H // NQ

PK_TDS, PK_TMK, PK_TMV, PK_TMR = 0, 8, 16, 24
PK_G, PK_B, PK_BR, PK_IOTA = 32, 40, 48, 56
PK_MASK, PK_ONES, PK_B1 = 64, 65, 193
PK_EPS5 = 193 + NE * 32
PK_EPS8 = PK_EPS5 + 1
PK_ONE = PK_EPS5 + 2
PK_W = PK_EPS5 + 3


class Emit:
    ENGS = ("t", "v", "s", "g", "y")

    def __init__(self, nc):
        self.nc = nc
        self.q = {e: [] for e in self.ENGS}
        self.sem = {}
        self.cnt = {}

    def new_sem(self, name, stack):
        self.sem[name] = stack.enter_context(self.nc.semaphore(name))
        self.cnt[name] = 0

    def op(self, eng, fn, inc=None, k=1):
        if inc is None:
            self.q[eng].append(fn)
            return None
        self.q[eng].append(fn)
        return self.inc_last(eng, inc, k)

    def inc_last(self, eng, sem, k=1):
        self.cnt[sem] += k
        val = self.cnt[sem]
        h = self.sem[sem]
        f = self.q[eng].pop()
        self.q[eng].append(lambda E, f=f, h=h, k=k: f(E).then_inc(h, k))
        return val

    def wait(self, eng, sem, val):
        if val is None or val <= 0:
            return
        h = self.sem[sem]
        self.q[eng].append(lambda E, h=h, v=val: E.wait_ge(h, v))

    def play(self, block):
        def run(E, lst):
            for f in lst:
                f(E)
        block.tensor(lambda E: run(E, self.q["t"]))
        block.vector(lambda E: run(E, self.q["v"]))
        block.scalar(lambda E: run(E, self.q["s"]))
        block.gpsimd(lambda E: run(E, self.q["g"]))
        block.sync(lambda E: run(E, self.q["y"]))

    def clear(self):
        self.q = {e: [] for e in self.ENGS}


def g3(t, ginner):
    return t[:, :].rearrange("p (g c) -> p g c", c=ginner)


def build_program(n_warm=2, n_real=16, dbg=False):
    NCH = n_warm + n_real
    ROWS = NCH * CH
    T = n_real * CH
    NT = n_real
    TCK = 512 if T % 512 == 0 else T
    NTC = T // TCK
    NML = TCK // 128

    nc = bass.Bass("TRN2", target_bir_lowering=False)

    x_in = nc.dram_tensor("x", [ROWS, C], F32, kind="ExternalInput").ap()
    wk_in = nc.dram_tensor("wk", [C, C], F32, kind="ExternalInput").ap()
    wv_in = nc.dram_tensor("wv", [C, C], F32, kind="ExternalInput").ap()
    wr_in = nc.dram_tensor("wr", [C, C], F32, kind="ExternalInput").ap()
    wo_in = nc.dram_tensor("wo", [C, C], F32, kind="ExternalInput").ap()
    w1_in = nc.dram_tensor("w1", [NE, C, H], F32, kind="ExternalInput").ap()
    w2_in = nc.dram_tensor("w2", [NE, H, C], F32, kind="ExternalInput").ap()
    b2_in = nc.dram_tensor("b2", [NE, C], F32, kind="ExternalInput").ap()
    wrf_in = nc.dram_tensor("wrf", [CG, 128, NE], F32, kind="ExternalInput").ap()
    pk_in = nc.dram_tensor("pk", [128, PK_W], F32, kind="ExternalInput").ap()
    eye_in = nc.dram_tensor("eye", [128, 128], F32, kind="ExternalInput").ap()
    y_out = nc.dram_tensor("y", [T, C], F32, kind="ExternalOutput").ap()
    wscr = nc.dram_tensor("wscr", [4, C, C], BF16, kind="Internal").ap()
    dbg_out = {}
    if dbg:
        for nm, shape, dt_ in (
                ("d_xnT", [128, CG * T], BF16), ("d_xlnT", [128, CG * (CH + 1)], BF16),
                ("d_mixk", [128, CG * CH], BF16), ("d_E", [128, CG * CH], F32),
                ("d_V", [128, CG * CH], F32), ("d_stD", [128, CG], F32),
                ("d_stN", [128, CG], F32), ("d_rw", [128, 16 * NE], F32),
                ("d_G", [128, CG * CH], BF16), ("d_WkT", [128, CG * C], BF16),
                ("d_stats", [128, 64], F32), ("d_xlnat", [128, C], BF16),
                ("d_tdb", [128, CG * CH], F32), ("d_wrfb", [128, CG * NE], BF16),
                ("d_tm1m", [128, 32], F32), ("d_sc", [128, 4 * NE], F32)):
            dbg_out[nm] = nc.dram_tensor(nm, shape, dt_, kind="ExternalOutput").ap()

    import contextlib
    stack = contextlib.ExitStack()
    with stack:
        em = Emit(nc)
        for s in ("sx", "sxb", "sc", "sw", "sv", "ss", "st", "so", "sg"):
            em.new_sem(s, stack)

        A = nc.alloc_sbuf_tensor
        acc = A("acc", [128, NT * C], F32)
        xnT = A("xnT", [128, CG * T], BF16)
        pk = A("pkc", [128, PK_W], F32)
        eyeb = A("eyeb", [128, 128], BF16)
        tdb = A("tdb", [128, CG * CH], F32)
        rw_all = A("rw_all", [128, NT * NE], F32)
        rwT = A("rwT", [8, NT * 128], BF16)
        b2bf = A("b2bf", [8, C], BF16)
        wrfb = A("wrfb", [128, CG * NE], BF16)
        scr8 = A("scr8", [128, 256], F32)
        tm1m = A("tm1m", [128, 32], F32)
        trsh = A("trsh", [128, 8], F32)
        rwb = A("rwb", [128, 4 * NE], BF16)
        sc_f = A("sc_f", [128, 4 * NE], F32)

        tds_c = lambda g: pk[:, PK_TDS + g:PK_TDS + g + 1]
        tm_c = {"k": (lambda g: pk[:, PK_TMK + g:PK_TMK + g + 1]),
                "v": (lambda g: pk[:, PK_TMV + g:PK_TMV + g + 1]),
                "r": (lambda g: pk[:, PK_TMR + g:PK_TMR + g + 1])}
        g_c = lambda g: pk[:, PK_G + g:PK_G + g + 1]
        b_c = lambda g: pk[:, PK_B + g:PK_B + g + 1]
        brt = pk[:, PK_BR:PK_BR + NE]
        iota8 = pk[:, PK_IOTA:PK_IOTA + NE]
        mask_c = pk[:, PK_MASK:PK_MASK + 1]
        ones128 = pk[:, PK_ONES:PK_ONES + 128]
        b1_c = lambda e, ht32: pk[:, PK_B1 + e * 32 + ht32:PK_B1 + e * 32 + ht32 + 1]
        tm_t = {"k": pk[:, PK_TMK:PK_TMK + 8], "v": pk[:, PK_TMV:PK_TMV + 8],
                "r": pk[:, PK_TMR:PK_TMR + 8]}
        PIX = {"k": 0, "v": 1, "r": 2}

        def bump(eng, sem):
            if eng == "s":
                return em.op("s", lambda E: E.activation(
                    trsh[:, 0:1], pk[:, 0:1], AF.Copy), inc=sem)
            return em.op("v", lambda E: E.tensor_copy(
                trsh[:, 4:5], pk[:, 0:1]), inc=sem)
        eps5 = pk[:, PK_EPS5:PK_EPS5 + 1]
        eps8 = pk[:, PK_EPS8:PK_EPS8 + 1]
        onec = pk[:, PK_ONE:PK_ONE + 1]

        import contextlib as _ctxlib
        attn_stack = _ctxlib.ExitStack()
        AE = lambda nm, shp, dt_: attn_stack.enter_context(
            nc.sbuf_tensor(nm, shp, dt_))
        WT = {}
        for w in ("k", "v", "r", "o"):
            WT[w] = AE(f"W{w}T", [128, CG * C], BF16)
        xlnTb = [AE(f"xlnT{i}", [128, CG * (CH + 1)], BF16) for i in range(2)]
        mix = {"k": AE("mixk", [128, CG * CH], BF16),
               "v": AE("mixv", [128, CG * CH], BF16)}
        mix["r"] = mix["k"]
        Gbuf = AE("Gbuf", [128, CG * CH], BF16)
        xlnat = AE("xlnat", [128, C], BF16)
        xnat = AE("xnat", [128, C], BF16)
        E2 = [AE(f"Ebuf{i}", [128, CG * CH], F32) for i in range(2)]
        V2 = [AE(f"Vbuf{i}", [128, CG * CH], F32) for i in range(2)]
        dead = AE("dead", [128, 512], F32)
        stats = AE("stats", [128, 64], F32)
        stD = AE("stD", [128, CG], F32)
        stN = AE("stN", [128, CG], F32)

        pm = [stack.enter_context(nc.psum_tensor(f"pm{i}", [128, 512], F32))
              for i in range(8)]
        ptrA = pm[0][:, :].bitcast(BF16)
        ptrB = pm[1][:, :].bitcast(BF16)
        pbf7 = pm[7][:, :].bitcast(BF16)

        G_ = Gbuf

        em.op("y", lambda E: E.dma_start(out=pk[:, :], in_=pk_in), inc="sc", k=16)
        sx_x = {}
        msF = {}

        em.op("g", lambda E: E.dma_start(out=eyeb[:, :], in_=eye_in), inc="sw", k=16)
        em.op("g", lambda E: E.dma_start(
            out=g3(wrfb, NE)[:, :, :],
            in_=wrf_in.rearrange("g p e -> p g e")), inc="sw", k=16)
        em.op("g", lambda E: E.dma_start(out=b2bf[:, :], in_=b2_in), inc="sw", k=16)
        sw_base = em.cnt["sw"]

        win = {"k": wk_in, "v": wv_in, "r": wr_in, "o": wo_in}
        ms_wcast = {}
        ms_wts_done = {}
        for wi, wname in enumerate(("k", "v", "r", "o")):
            ms_wcast[wname] = em.op(
                "g", lambda E, wi=wi, wname=wname: E.dma_start(
                    out=wscr[wi], in_=win[wname]), inc="sw", k=16)
        for wi, wname in enumerate(("k", "v", "r", "o")):
            em.wait("s", "sw", ms_wcast[wname])
            for cg in range(CG):
                em.op("s", lambda E, wi=wi, wname=wname, cg=cg:
                      E.dma_start(out=g3(WT[wname], C)[:, cg, :],
                                  in_=wscr[wi][:, cg * 128:(cg + 1) * 128],
                                  transpose=True), inc="sw", k=16)
            ms_wts_done[wname] = em.cnt["sw"]

        em.wait("s", "sc", 16)
        em.op("s", lambda E: E.activation(
            tm1m[:, 24:32], pk[:, PK_TDS:PK_TDS + 8], AF.Exp))
        ms_etd = bump("s", "ss")
        em.wait("v", "sc", 16)
        em.wait("v", "ss", ms_etd)
        for g in range(CG):
            em.op("v", lambda E, g=g: E.tensor_scalar(
                tdb[:, g * CH:(g + 1) * CH], ones128[:, 0:CH],
                tm1m[:, 24 + g:24 + g + 1], None, ALU.mult))
        em.op("v", lambda E: E.tensor_scalar(
            stD[:, :], iota8, 0.0, None, ALU.mult))
        em.op("v", lambda E: E.tensor_scalar(
            stN[:, :], iota8, 0.0, None, ALU.mult))
        for pi, p in enumerate(("k", "v", "r")):
            em.op("v", lambda E, pi=pi, p=p: E.tensor_scalar(
                tm1m[:, pi * 8:pi * 8 + 8], tm_t[p], -1.0, 1.0,
                ALU.mult, ALU.add))

        STAT0 = lambda par, i: stats[:, (par % 4) * 16 + i:(par % 4) * 16 + i + 1]
        msA = {}; msG = {}; msH = {}; msI = {}
        msJk = {}; msJv = {}; msK = {}; msL = {}; msM = {}; msN = {}; msHs = {}
        msO = {}; msP = {}; msQ = {}; msR = {}; msV = {}
        bankfree23 = {}
        msXD = {}

        def emitA(c):
            real = c >= n_warm
            m = c - n_warm
            xin = acc[:, (m if real else n_real - n_warm + c) * C:
                      (m if real else n_real - n_warm + c) * C + C]
            STAT = lambda i: STAT0(c % 4, i)

            if real and c - n_real >= 0:
                em.wait("y", "sv", msF[c - n_real])
            xsem = "sx" if c % 2 == 0 else "sxb"
            em.op("y", lambda E, xr=x_in[c * CH:(c + 1) * CH, :], xd=xin:
                  E.dma_start(out=xd, in_=xr), inc=xsem, k=16)
            msXD[c] = (xsem, em.cnt[xsem])

            em.wait("s", msXD[c][0], msXD[c][1])
            if c >= 4:
                em.wait("s", "sv", msF[c - 4])
            if c >= 1:
                em.wait("s", "st", msG[c - 1])
            em.op("s", lambda E, xd=xin: E.activation(
                xnat[:, :], xd, AF.Copy, accum_out=STAT(0)))
            em.op("s", lambda E, xd=xin: E.activation(
                xnat[:, :], xd, AF.Square, accum_out=STAT(1)))
            em.op("s", lambda E: E.mul(STAT(2), STAT(0), -1.0 / C))
            em.op("s", lambda E: E.activation(STAT(4), STAT(2), AF.Square))
            em.op("s", lambda E: E.mul(STAT(5), STAT(4), -1.0))
            em.op("s", lambda E: E.mul(STAT(3), STAT(1), 1.0 / C))
            em.op("s", lambda E: E.add(STAT(6), STAT(3), STAT(5)))
            em.op("s", lambda E: E.activation(STAT(7), STAT(6), AF.Ln, bias=eps5))
            em.op("s", lambda E: E.activation(
                STAT(8), STAT(7), AF.Exp, scale=-0.5))
            em.op("s", lambda E: E.activation(STAT(10), STAT(6), AF.Ln))
            em.op("s", lambda E: E.mul(STAT(9), STAT(7), -1.0))
            em.op("s", lambda E: E.mul(STAT(14), STAT(7), -0.5))
            em.op("s", lambda E: E.activation(STAT(12), STAT(10), AF.Exp,
                                              bias=STAT(9)))
            em.op("s", lambda E: E.activation(STAT(13), STAT(12), AF.Ln, bias=eps5))
            em.op("s", lambda E: E.activation(
                STAT(15), STAT(13), AF.Exp, scale=-0.5, bias=STAT(14)))
            msA[c] = bump("s", "ss")

            em.wait("v", "ss", msA[c])
            if c >= 1:
                em.wait("v", "st", msG[c - 1])
            fn = lambda E, xd=xin: E.tensor_scalar(
                xlnat[:, :], xd, STAT(2), STAT(15), ALU.add, ALU.mult)
            if real:
                em.op("v", fn)
                msF[c] = em.op("v", lambda E, xd=xin: E.tensor_scalar(
                    xnat[:, :], xd, STAT(2), STAT(8), ALU.add, ALU.mult),
                    inc="sv")
            else:
                msF[c] = em.op("v", fn, inc="sv")

            em.wait("t", "sv", msF[c])
            if c >= 1:
                em.wait("t", "sv", msH[c - 1])
                if (c - 1) in msHs:
                    em.wait("t", "ss", msHs[c - 1])
            if real:
                for cg in range(CG):
                    em.op("t", lambda E, cg=cg: E.transpose(
                        ptrA[:, cg * 128:(cg + 1) * 128],
                        xnat[:, cg * 128:(cg + 1) * 128], eyeb[:, :]))
            for cg in range(CG):
                fn = lambda E, cg=cg: E.transpose(
                    ptrB[:, cg * 128:(cg + 1) * 128],
                    xlnat[:, cg * 128:(cg + 1) * 128], eyeb[:, :])
                if cg == CG - 1:
                    msG[c] = em.op("t", fn, inc="st")
                else:
                    em.op("t", fn)

            em.wait("v", "st", msG[c])
            xlnT3 = g3(xlnTb[c % 2], CH + 1)
            if c == 0:
                em.op("v", lambda E: E.tensor_scalar(
                    xlnT3[:, :, 0], iota8[:, :], 0.0, None, ALU.mult))
            else:
                prev3 = g3(xlnTb[(c - 1) % 2], CH + 1)
                em.op("v", lambda E, p3=prev3: E.tensor_copy(
                    xlnT3[:, :, 0], p3[:, :, CH]))
            if real:
                em.wait("s", "st", msG[c])
                em.op("s", lambda E, m=m: E.copy(
                    g3(xnT, T)[:, :, m * CH:(m + 1) * CH],
                    ptrA.rearrange("p (g c) -> p g c", c=128)))
                msHs[c] = em.inc_last("s", "ss")
            for cg in range(CG):
                fn = lambda E, cg=cg: E.tensor_scalar(
                    xlnT3[:, cg, 1:CH + 1], ptrB[:, cg * 128:(cg + 1) * 128],
                    g_c(cg), b_c(cg), ALU.mult, ALU.add)
                if cg == CG - 1:
                    msH[c] = em.op("v", fn, inc="sv")
                else:
                    em.op("v", fn)

        def emitB(c):
            real = c >= n_warm
            m = c - n_warm
            E_ = E2[c % 2]
            V_ = V2[c % 2]
            xlnT3 = g3(xlnTb[c % 2], CH + 1)

            def emit_mix(p):
                for cg in range(CG):
                    em.op("v", lambda E, p=p, cg=cg: E.tensor_scalar(
                        mix[p][:, cg * CH:(cg + 1) * CH],
                        xlnT3[:, cg, 1:CH + 1], tm_c[p](cg), None, ALU.mult))
                for cg in range(CG):
                    em.op("v", lambda E, p=p, cg=cg: E.scalar_tensor_tensor(
                        mix[p][:, cg * CH:(cg + 1) * CH],
                        xlnT3[:, cg, 0:CH],
                        tm1m[:, PIX[p] * 8 + cg:PIX[p] * 8 + cg + 1],
                        mix[p][:, cg * CH:(cg + 1) * CH],
                        ALU.mult, ALU.add))

            def emit_patch(pi, p):
                em.op("v", lambda E, p=p, pi=pi: E.tensor_mul(
                    scr8[:, 16 + 8 * pi:24 + 8 * pi], scr8[:, 8:16], tm_t[p]))
                em.op("v", lambda E: E.tensor_copy(trsh[:, 4:5], pk[:, 0:1]))
                em.op("v", lambda E, p=p, pi=pi: E.tensor_add(
                    g3(mix[p], CH)[:, :, 0], scr8[:, 16 + 8 * pi:24 + 8 * pi],
                    scr8[:, 0:8]))

            em.wait("v", "st", msG[c])
            if c >= 1:
                if (c - 1) in msL:
                    em.wait("v", "st", msL[c - 1])
                em.wait("v", "st", msJv[c - 1])
            if c == n_warm:
                em.op("v", lambda E: E.tensor_scalar(
                    scr8[:, 0:8], xlnT3[:, :, 0], mask_c, None, ALU.mult))
                em.op("v", lambda E: E.tensor_copy(trsh[:, 4:5], pk[:, 0:1]))
                em.op("v", lambda E: E.tensor_sub(
                    scr8[:, 8:16], xlnT3[:, :, 1], scr8[:, 0:8]))
                em.op("v", lambda E: E.tensor_copy(trsh[:, 4:5], pk[:, 0:1]))
            emit_mix("k")
            emit_mix("v")
            if c == n_warm:
                emit_patch(0, "k")
                emit_patch(1, "v")
            msI[c] = em.inc_last("v", "sv")

            em.wait("t", "sv", msI[c])
            if c >= 1:
                em.wait("t", "ss", bankfree23[c - 1])
                em.wait("t", "ss", msV[c - 1])
            if (c - 2) in msQ:
                em.wait("t", "sv", msQ[c - 2])
            if c == 0:
                em.wait("t", "sw", ms_wts_done["v"])
            for (p, bofs, msd) in (("k", 2, msJk), ("v", 4, msJv)):
                for coT in range(CG):
                    bank = pm[bofs + coT // 4]
                    cofs = (coT % 4) * 128
                    for cg in range(CG):
                        fn = (lambda E, p=p, coT=coT, cg=cg, b=bank, co=cofs:
                              E.matmul(b[:, co:co + 128],
                                       g3(WT[p], C)[:, cg, coT * 128:(coT + 1) * 128],
                                       g3(mix[p], CH)[:, cg, :],
                                       start=(cg == 0), stop=(cg == CG - 1)))
                        if coT == CG - 1 and cg == CG - 1:
                            msd[c] = em.op("t", fn, inc="st")
                        else:
                            em.op("t", fn)

            em.wait("s", "st", msJk[c])
            if c >= 2:
                em.wait("s", "sg", msP[c - 2])
            em.op("s", lambda E: E.activation(E_[:, 0:512], pm[2][:, :], AF.Exp))
            em.op("s", lambda E: E.activation(E_[:, 512:1024], pm[3][:, :], AF.Exp))
            em.wait("s", "st", msJv[c])
            em.op("s", lambda E: E.copy(V_[:, 0:512], pm[4][:, :]))
            msK[c] = em.op("s", lambda E: E.copy(
                V_[:, 512:1024], pm[5][:, :]), inc="ss")
            msV[c] = msK[c]
            bankfree23[c] = msK[c]

            if real:
                em.wait("v", "st", msJk[c])
                emit_mix("r")
                if c == n_warm:
                    emit_patch(2, "r")
                msIr = em.inc_last("v", "sv")

            if real:
                em.wait("t", "sv", msIr)
                if (c - 1) in msO:
                    em.wait("t", "ss", msO[c - 1])
                if c == n_warm:
                    em.wait("t", "sw", ms_wts_done["r"])
                for coT in range(CG):
                    bank = pm[6 + coT // 4]
                    cofs = (coT % 4) * 128
                    for cg in range(CG):
                        fn = (lambda E, coT=coT, cg=cg, b=bank, co=cofs:
                              E.matmul(b[:, co:co + 128],
                                       g3(WT["r"], C)[:, cg, coT * 128:(coT + 1) * 128],
                                       g3(mix["r"], CH)[:, cg, :],
                                       start=(cg == 0), stop=(cg == CG - 1)))
                        if coT == CG - 1 and cg == CG - 1:
                            msL[c] = em.op("t", fn, inc="st")
                        else:
                            em.op("t", fn)

            em.wait("g", "ss", msK[c])
            msKV = em.op("g", lambda E: E.tensor_mul(
                V_[:, :], E_[:, :], V_[:, :]), inc="sg")
            em.wait("v", "ss", msK[c])
            em.wait("v", "sg", msKV)
            for cg in range(CG):
                sl = slice(cg * CH, (cg + 1) * CH)
                em.op("v", lambda E, sl=sl, cg=cg: E.tensor_tensor_scan(
                    E_[:, sl], tdb[:, sl], E_[:, sl], stD[:, cg:cg + 1],
                    ALU.mult, ALU.add))
            for cg in range(CG):
                sl = slice(cg * CH, (cg + 1) * CH)
                em.op("v", lambda E, sl=sl, cg=cg: E.tensor_tensor_scan(
                    V_[:, sl], tdb[:, sl], V_[:, sl], stN[:, cg:cg + 1],
                    ALU.mult, ALU.add))
            em.op("v", lambda E: E.tensor_copy(stD[:, :], g3(E_, CH)[:, :, CH - 1]))
            em.op("v", lambda E: E.tensor_copy(trsh[:, 4:5], pk[:, 0:1]))
            msM[c] = em.op("v", lambda E: E.tensor_copy(
                stN[:, :], g3(V_, CH)[:, :, CH - 1]), inc="sv")


        def emitBback(c):
            real = c >= n_warm
            E_ = E2[c % 2]
            V_ = V2[c % 2]

            em.wait("s", "sv", msM[c])
            for hh in range(2):
                em.op("s", lambda E, hh=hh: E.activation(
                    dead[:, :], E_[:, hh * 512:(hh + 1) * 512], AF.Ln,
                    bias=eps8))
                fn = lambda E, hh=hh: E.activation(
                    E_[:, hh * 512:(hh + 1) * 512], dead[:, :], AF.Exp,
                    scale=-1.0)
                if hh == 1:
                    msN[c] = em.op("s", fn, inc="ss")
                else:
                    em.op("s", fn)

            if real:
                em.wait("s", "st", msL[c])
                if (c - 1) in msR:
                    em.wait("s", "st", msR[c - 1])
                for bi, b in enumerate((6, 7)):
                    em.op("s", lambda E, b=b: E.activation(
                        dead[:, 0:512], pm[b][:, :], AF.Exp, scale=-1.0))
                    em.op("s", lambda E: E.activation(
                        dead[:, 0:512], dead[:, 0:512], AF.Ln, bias=onec))
                    fn = lambda E, bi=bi: E.activation(
                        G_[:, bi * 512:(bi + 1) * 512], dead[:, 0:512],
                        AF.Exp, scale=-1.0)
                    if bi == 1:
                        msO[c] = em.op("s", fn, inc="ss")
                    else:
                        em.op("s", fn)

            em.wait("g", "ss", msN[c])
            em.wait("g", "sv", msM[c])
            fn = lambda E: E.tensor_mul(V_[:, :], V_[:, :], E_[:, :])
            if real:
                em.op("g", fn)
                em.wait("g", "ss", msO[c])
                msP[c] = em.op("g", lambda E: E.tensor_mul(
                    G_[:, :], V_[:, :], G_[:, :]), inc="sg")
            else:
                msP[c] = em.op("g", fn, inc="sg")

        def emitWo(c, free_ss):
            if c < n_warm:
                return
            m = c - n_warm
            em.wait("t", "sg", msP[c])
            em.wait("t", "ss", free_ss)
            if (c - 1) in msQ:
                em.wait("t", "sv", msQ[c - 1])
            if c == n_warm:
                em.wait("t", "sw", ms_wts_done["o"])
            for half in range(2):
                for cg in range(CG):
                    fn = (lambda E, half=half, cg=cg:
                          E.matmul(pm[2 + half][:, :],
                                   g3(G_, CH)[:, cg, :],
                                   g3(WT["o"], C)[:, cg,
                                                  half * 512:(half + 1) * 512],
                                   start=(cg == 0), stop=(cg == CG - 1)))
                    if half == 1 and cg == CG - 1:
                        msR[c] = em.op("t", fn, inc="st")
                    else:
                        em.op("t", fn)

        def emitAcc(c):
            if c < n_warm:
                return
            m = c - n_warm
            em.wait("v", "st", msR[c])
            em.op("v", lambda E, m=m: E.tensor_add(
                acc[:, m * C:m * C + 512], acc[:, m * C:m * C + 512],
                pm[2][:, :]))
            msQ[c] = em.op("v", lambda E, m=m: E.tensor_add(
                acc[:, m * C + 512:m * C + 1024],
                acc[:, m * C + 512:m * C + 1024], pm[3][:, :]), inc="sv")

        emitA(0)
        for c in range(NCH):
            if c + 1 < NCH:
                emitA(c + 1)
            if c >= 1:
                emitBback(c - 1)
            emitB(c)
            if c >= 1:
                emitWo(c - 1, msK[c])
                emitAcc(c - 1)
        emitBback(NCH - 1)
        emitWo(NCH - 1, msK[NCH - 1])
        emitAcc(NCH - 1)

        last = NCH - 1
        attn_v_done = em.cnt["sv"]
        attn_s_done = em.cnt["ss"]

        ms_b2 = {}

        GL = 4 if NT % 4 == 0 else (2 if NT % 2 == 0 else 1)
        NG = NT // GL
        ms_grp_read = {}
        em.wait("t", "sv", msQ[last])
        em.wait("t", "ss", attn_s_done)
        LC = lambda l, a, b: scr8[:, l * 64 + a:l * 64 + b]
        SCL = lambda l: sc_f[:, l * NE:(l + 1) * NE]
        for grp in range(NG):
            mis = [grp * GL + l for l in range(GL)]
            if grp >= 1:
                em.wait("t", "sv", ms_grp_read[grp - 1])
            for l, mi in enumerate(mis):
                for cg in range(CG):
                    em.op("t", lambda E, l=l, mi=mi, cg=cg: E.matmul(
                        pm[6][:, l * NE:l * NE + NE],
                        g3(xnT, T)[:, cg, mi * CH:mi * CH + 128],
                        g3(wrfb, NE)[:, cg, :],
                        start=(cg == 0), stop=(cg == CG - 1)))
            ms_scg = em.inc_last("t", "st")
            em.wait("v", "st", ms_scg)

            def lanes(fn):
                for l in range(GL):
                    em.op("v", lambda E, l=l: fn(E, l))
            lanes(lambda E, l: E.tensor_add(
                SCL(l), pm[6][:, l * NE:l * NE + NE], brt))
            ms_grp_read[grp] = bump("v", "sv")
            lanes(lambda E, l: E.tensor_scalar(
                SCL(l), SCL(l), 0.0, None, ALU.max))
            lanes(lambda E, l: E.reduce_max(LC(l, 0, 1), SCL(l), axis=AX.X))
            lanes(lambda E, l: E.tensor_scalar(
                LC(l, 8, 16), SCL(l), LC(l, 0, 1), None, ALU.is_equal))
            lanes(lambda E, l: E.tensor_scalar(
                LC(l, 8, 16), LC(l, 8, 16), -1e9, 1e9, ALU.mult, ALU.add))
            lanes(lambda E, l: E.tensor_add(LC(l, 8, 16), LC(l, 8, 16), iota8))
            lanes(lambda E, l: E.tensor_reduce(
                LC(l, 1, 2), LC(l, 8, 16), axis=AX.X, op=ALU.min))
            lanes(lambda E, l: E.tensor_scalar(
                LC(l, 16, 24), iota8, LC(l, 1, 2), None, ALU.is_equal))
            lanes(lambda E, l: E.scalar_tensor_tensor(
                LC(l, 24, 32), LC(l, 16, 24), -1e9, SCL(l), ALU.mult, ALU.add))
            lanes(lambda E, l: E.reduce_max(LC(l, 2, 3), LC(l, 24, 32), axis=AX.X))
            lanes(lambda E, l: E.tensor_scalar(
                LC(l, 32, 40), LC(l, 24, 32), LC(l, 2, 3), None, ALU.is_equal))
            lanes(lambda E, l: E.tensor_scalar(
                LC(l, 32, 40), LC(l, 32, 40), -1e9, 1e9, ALU.mult, ALU.add))
            lanes(lambda E, l: E.tensor_add(LC(l, 32, 40), LC(l, 32, 40), iota8))
            lanes(lambda E, l: E.tensor_reduce(
                LC(l, 3, 4), LC(l, 32, 40), axis=AX.X, op=ALU.min))
            lanes(lambda E, l: E.tensor_scalar(
                LC(l, 40, 48), iota8, LC(l, 3, 4), None, ALU.is_equal))
            lanes(lambda E, l: E.tensor_sub(LC(l, 4, 5), LC(l, 2, 3), LC(l, 0, 1)))
            ms_d = bump("v", "sv")
            em.wait("s", "sv", ms_d)
            for l in range(GL):
                em.op("s", lambda E, l=l: E.activation(
                    LC(l, 5, 6), LC(l, 4, 5), AF.Exp))
            ms_e2 = bump("s", "ss")
            em.wait("v", "ss", ms_e2)
            lanes(lambda E, l: E.tensor_scalar(
                LC(l, 6, 7), LC(l, 5, 6), 1.0, None, ALU.add))
            lanes(lambda E, l: E.reciprocal(LC(l, 6, 7), LC(l, 6, 7)))
            lanes(lambda E, l: E.tensor_mul(LC(l, 7, 8), LC(l, 5, 6), LC(l, 6, 7)))
            lanes(lambda E, l: E.tensor_scalar(
                LC(l, 48, 56), LC(l, 16, 24), LC(l, 6, 7), None, ALU.mult))
            lanes(lambda E, l: E.tensor_scalar(
                LC(l, 56, 64), LC(l, 40, 48), LC(l, 7, 8), None, ALU.mult))
            for l, mi in enumerate(mis):
                em.op("v", lambda E, l=l, mi=mi: E.tensor_add(
                    rw_all[:, mi * NE:(mi + 1) * NE],
                    LC(l, 48, 56), LC(l, 56, 64)))
            for l, mi in enumerate(mis):
                em.op("v", lambda E, l=l, mi=mi: E.tensor_copy(
                    rwb[:, l * NE:(l + 1) * NE],
                    rw_all[:, mi * NE:(mi + 1) * NE]))
            ms_rwbg = bump("v", "sv")
            em.wait("t", "sv", ms_rwbg)
            for l in range(GL):
                em.op("t", lambda E, l=l: E.transpose(
                    pbf7[0:8, l * 128:(l + 1) * 128],
                    rwb[:, l * NE:(l + 1) * NE], eyeb[:, :]))
            ms_rwtg = em.inc_last("t", "st")
            em.wait("v", "st", ms_rwtg)
            for l, mi in enumerate(mis):
                em.op("v", lambda E, l=l, mi=mi: E.tensor_copy(
                    rwT[:, mi * 128:(mi + 1) * 128],
                    pbf7[0:8, l * 128:(l + 1) * 128]))
            ms_cpg = bump("v", "sv")
            em.wait("t", "sv", ms_cpg)
            for l, mi in enumerate(mis):
                for half in range(2):
                    msb = em.op("t", lambda E, mi=mi, half=half: E.matmul(
                        pm[4 + half][:, :], rwT[:, mi * 128:(mi + 1) * 128],
                        b2bf[:, half * 512:(half + 1) * 512],
                        start=True, stop=True), inc="st")
                    em.wait("v", "st", msb)
                    ms_b2[(mi, half)] = em.op(
                        "v", lambda E, mi=mi, half=half: E.tensor_add(
                            acc[:, mi * C + half * 512:mi * C + half * 512 + 512],
                            acc[:, mi * C + half * 512:mi * C + half * 512 + 512],
                            pm[4 + half][:, :]), inc="sv")
                    em.wait("t", "sv", ms_b2[(mi, half)])

        router_v_done = em.cnt["sv"]
        router_s_done = em.cnt["ss"]
        router_t_done = em.cnt["st"]

        with nc.Block() as blk1:
            em.play(blk1)
        em.clear()
        if not dbg:
            attn_stack.close()
        else:
            stack.enter_context(attn_stack)
        w1q = [stack.enter_context(nc.sbuf_tensor(f"w1q{i}", [128, CG * HQ],
                                                  BF16)) for i in range(2)]
        w2q = [stack.enter_context(nc.sbuf_tensor(f"w2q{i}", [128, CG * HQ],
                                                  BF16)) for i in range(2)]
        hT = [stack.enter_context(nc.sbuf_tensor(f"hT{i}", [128, CG * TCK],
                                                 BF16)) for i in range(2)]

        ms_wffn = {}
        ms_ffn_t = {}

        def emit_ffn_dma(i):
            e, q = divmod(i, NQ)
            if i >= 2:
                em.wait("g", "st", ms_ffn_t[i - 2])
            elif i == 0:
                em.wait("g", "st", msL[NCH - 1])
                em.wait("g", "sc", 16)
            else:
                em.wait("g", "st", msR[NCH - 1])
                em.wait("g", "sc", 16)
            src1 = w1_in[e, :, q * HQ:(q + 1) * HQ] \
                .rearrange("(g p) h -> p g h", p=128)
            src2 = w2_in[e, q * HQ:(q + 1) * HQ, :] \
                .rearrange("(g p) d -> p g d", p=128)
            em.op("g", lambda E, s=src1, i=i: E.dma_start(
                out=g3(w1q[i % 2], HQ)[:, :, :], in_=s), inc="sw", k=16)
            ms_wffn[i] = em.op("g", lambda E, s=src2, i=i: E.dma_start(
                out=g3(w2q[i % 2], HQ)[:, :, :], in_=s), inc="sw", k=16)

        emit_ffn_dma(0)
        emit_ffn_dma(1)

        em.wait("t", "sv", router_v_done)
        em.wait("t", "ss", attn_s_done)
        bank_free_h = {}
        bank_free_y = {}
        hT_reader = {}
        h_done = {}
        NJ = NE * NQ * NTC

        def emit_h(k):
            i, n = divmod(k, NTC)
            e, q = divmod(i, NQ)
            if n == 0:
                em.wait("t", "sw", ms_wffn[i])
            for ht in range(CG):
                b = 2 + (ht % 2)
                if b in bank_free_h:
                    em.wait("t", "ss", bank_free_h[b])
                for cg in range(CG):
                    fn = (lambda E, i=i, n=n, ht=ht, cg=cg, b=b:
                          E.matmul(pm[b][:, 0:TCK],
                                   g3(w1q[i % 2], HQ)[:, cg,
                                                      ht * 128:(ht + 1) * 128],
                                   g3(xnT, T)[:, cg, n * TCK:(n + 1) * TCK],
                                   start=(cg == 0), stop=(cg == CG - 1)))
                    em.op("t", fn)
                ms_h = em.inc_last("t", "st")
                em.wait("s", "st", ms_h)
                if ht == 0 and (k % 2) in hT_reader:
                    em.wait("s", "st", hT_reader[k % 2])
                bank_free_h[b] = em.op(
                    "s", lambda E, e=e, q=q, k=k, ht=ht, b=b: E.activation(
                        g3(hT[k % 2], TCK)[:, ht, :], pm[b][:, 0:TCK],
                        AF.Relu, bias=b1_c(e, q * 8 + ht)), inc="ss")
            h_done[k] = em.cnt["ss"]

        def emit_y(k):
            i, n = divmod(k, NTC)
            e, q = divmod(i, NQ)
            em.wait("t", "ss", h_done[k])
            for ml in range(NML):
                mi = n * NML + ml
                for half in range(2):
                    b = 4 + half
                    if b in bank_free_y:
                        em.wait("t", "sv", bank_free_y[b])
                    for hg in range(CG):
                        fn = (lambda E, i=i, k=k, ml=ml, half=half, hg=hg, b=b:
                              E.matmul(pm[b][:, :],
                                       g3(hT[k % 2], TCK)[:, hg,
                                                          ml * 128:(ml + 1) * 128],
                                       g3(w2q[i % 2], HQ)[:, hg,
                                                          half * 512:(half + 1) * 512],
                                       start=(hg == 0), stop=(hg == CG - 1)))
                        em.op("t", fn)
                    ms_y = em.inc_last("t", "st")
                    hT_reader[k % 2] = ms_y
                    em.wait("v", "st", ms_y)
                    bank_free_y[b] = em.op(
                        "v", lambda E, e=e, mi=mi, half=half, b=b:
                        E.scalar_tensor_tensor(
                            acc[:, mi * C + half * 512:mi * C + half * 512 + 512],
                            pm[b][:, :],
                            rw_all[:, mi * NE + e:mi * NE + e + 1],
                            acc[:, mi * C + half * 512:mi * C + half * 512 + 512],
                            ALU.mult, ALU.add), inc="sv")

        emit_h(0)
        for k in range(NJ):
            if k + 1 < NJ:
                emit_h(k + 1)
            emit_y(k)
            i, n = divmod(k, NTC)
            if n == NTC - 1:
                ms_ffn_t[i] = em.cnt["st"]
                if i + 2 < NE * NQ:
                    emit_ffn_dma(i + 2)

        em.wait("y", "sv", em.cnt["sv"])
        em.wait("y", "ss", em.cnt["ss"])
        em.wait("y", "st", em.cnt["st"])
        if dbg:
            ndb = 0
            for nm, buf in (("d_xnT", xnT), ("d_xlnT", xlnTb[(NCH - 1) % 2]), ("d_mixk", mix["k"]),
                            ("d_E", E2[(NCH - 1) % 2]), ("d_V", V2[(NCH - 1) % 2]), ("d_stD", stD),
                            ("d_stN", stN), ("d_G", G_), ("d_WkT", WT["k"]),
                            ("d_stats", stats), ("d_xlnat", xlnat),
                            ("d_tdb", tdb), ("d_wrfb", wrfb),
                            ("d_tm1m", tm1m), ("d_sc", sc_f)):
                em.op("y", lambda E, o=dbg_out[nm], b=buf: E.dma_start(
                    out=o, in_=b[:, :]), inc="so", k=16)
                ndb += 1
            em.op("y", lambda E, o=dbg_out["d_rw"], b=rw_all: E.dma_start(
                out=o[:, 0:NT * NE], in_=b[:, :]), inc="so", k=16)
            ndb += 1
            em.wait("y", "so", 16 * ndb)
        em.op("y", lambda E: E.dma_start(
            out=y_out.rearrange("(m p) d -> p m d", p=128),
            in_=acc[:, :].rearrange("p (m d) -> p m d", d=C)), inc="so", k=16)
        em.wait("y", "so", 16)

        with nc.Block() as blk2:
            em.play(blk2)
        em.clear()

    return nc


def _shard_inputs(inputs, n_warm=2, n_real=16):
    x = np.ascontiguousarray(inputs["L0_attn_x"], dtype=np.float32)
    B, S, _ = x.shape
    T = n_real * CH
    PRE = n_warm * CH
    ncore_per_b = S // T

    def pack_row(a):
        return np.ascontiguousarray(a, dtype=np.float32).reshape(CG, 128).T

    pk = np.zeros((128, PK_W), np.float32)
    pk[:, PK_TDS:PK_TDS + 8] = pack_row(inputs["L0_attn_time_decay"])
    pk[:, PK_TMK:PK_TMK + 8] = pack_row(inputs["L0_attn_time_mix_k"].reshape(-1))
    pk[:, PK_TMV:PK_TMV + 8] = pack_row(inputs["L0_attn_time_mix_v"].reshape(-1))
    pk[:, PK_TMR:PK_TMR + 8] = pack_row(inputs["L0_attn_time_mix_r"].reshape(-1))
    pk[:, PK_G:PK_G + 8] = pack_row(inputs["L0_attn_ln_g"])
    pk[:, PK_B:PK_B + 8] = pack_row(inputs["L0_attn_ln_b"])
    pk[:, PK_BR:PK_BR + 8] = np.broadcast_to(
        np.asarray(inputs["L0_ffn_br"], np.float32), (128, 8))
    pk[:, PK_IOTA:PK_IOTA + 8] = np.broadcast_to(
        np.arange(8, dtype=np.float32), (128, 8))
    pk[:, PK_ONES:PK_ONES + 128] = 1.0
    b1 = np.asarray(inputs["L0_ffn_b1"], np.float32)
    pk[:, PK_B1:PK_B1 + 256] = b1.reshape(8, 32, 128).transpose(2, 0, 1) \
        .reshape(128, 256)
    pk[:, PK_EPS5] = 1e-5
    pk[:, PK_EPS8] = 1e-8
    pk[:, PK_ONE] = 1.0

    common = {
        "wk": np.ascontiguousarray(inputs["L0_attn_Wk"], np.float32),
        "wv": np.ascontiguousarray(inputs["L0_attn_Wv"], np.float32),
        "wr": np.ascontiguousarray(inputs["L0_attn_Wr"], np.float32),
        "wo": np.ascontiguousarray(inputs["L0_attn_Wo"], np.float32),
        "w1": np.ascontiguousarray(inputs["L0_ffn_W1"], np.float32),
        "w2": np.ascontiguousarray(inputs["L0_ffn_W2"], np.float32),
        "b2": np.ascontiguousarray(inputs["L0_ffn_b2"], np.float32),
        "wrf": np.ascontiguousarray(inputs["L0_ffn_Wr"], np.float32)
            .reshape(CG, 128, NE),
        "eye": np.eye(128, dtype=np.float32),
    }

    in_maps = []
    for b in range(B):
        for j in range(ncore_per_b):
            t0 = j * T
            if t0 - PRE >= 0:
                xs = x[b, t0 - PRE:t0 + T]
                msk = 1.0
            else:
                xs = np.concatenate(
                    [np.zeros((PRE - t0, C), np.float32), x[b, 0:t0 + T]], axis=0)
                msk = 0.0
            pkc = pk.copy()
            pkc[:, PK_MASK] = msk
            m = dict(common)
            m["x"] = np.ascontiguousarray(xs)
            m["pk"] = pkc
            in_maps.append(m)
    return in_maps


_NC_CACHE = {}


def kernel(**inputs):
    n_warm, n_real = 2, 16
    key = (n_warm, n_real)
    if key not in _NC_CACHE:
        _NC_CACHE[key] = build_program(n_warm, n_real)
    nc = _NC_CACHE[key]
    in_maps = _shard_inputs(inputs, n_warm, n_real)
    res = run_bass_kernel_spmd(nc, in_maps, core_ids=list(range(8)))
    x = inputs["L0_attn_x"]
    B, S, _ = x.shape
    T = n_real * CH
    out = np.empty((B, S, C), np.float32)
    ci = 0
    for b in range(B):
        for j in range(S // T):
            out[b, j * T:(j + 1) * T] = res.results[ci]["y"]
            ci += 1
    return out


# version 17
# speedup vs baseline: 1.0329x; 1.0325x over previous
import numpy as np
import ml_dtypes
import concourse.bass as bass
import concourse.mybir as mybir
from concourse.bass_utils import run_bass_kernel_spmd

F32 = mybir.dt.float32
BF16 = mybir.dt.bfloat16
AF = mybir.ActivationFunctionType
ALU = mybir.AluOpType
AX = mybir.AxisListType

C = 1024
CG = 8
CH = 128
H = 4096
NE = 8
NQ = 4
HQ = H // NQ

PK_TDS, PK_TMK, PK_TMV, PK_TMR = 0, 8, 16, 24
PK_G, PK_B, PK_BR, PK_IOTA = 32, 40, 48, 56
PK_MASK, PK_ONES, PK_B1 = 64, 65, 193
PK_EPS5 = 193 + NE * 32
PK_EPS8 = PK_EPS5 + 1
PK_ONE = PK_EPS5 + 2
PK_W = PK_EPS5 + 3


class Emit:
    ENGS = ("t", "v", "s", "g", "y")

    def __init__(self, nc):
        self.nc = nc
        self.q = {e: [] for e in self.ENGS}
        self.sem = {}
        self.cnt = {}

    def new_sem(self, name, stack):
        self.sem[name] = stack.enter_context(self.nc.semaphore(name))
        self.cnt[name] = 0

    def op(self, eng, fn, inc=None, k=1):
        if inc is None:
            self.q[eng].append(fn)
            return None
        self.q[eng].append(fn)
        return self.inc_last(eng, inc, k)

    def inc_last(self, eng, sem, k=1):
        self.cnt[sem] += k
        val = self.cnt[sem]
        h = self.sem[sem]
        f = self.q[eng].pop()
        self.q[eng].append(lambda E, f=f, h=h, k=k: f(E).then_inc(h, k))
        return val

    def wait(self, eng, sem, val):
        if val is None or val <= 0:
            return
        h = self.sem[sem]
        self.q[eng].append(lambda E, h=h, v=val: E.wait_ge(h, v))

    def play(self, block):
        def run(E, lst):
            for f in lst:
                f(E)
        block.tensor(lambda E: run(E, self.q["t"]))
        block.vector(lambda E: run(E, self.q["v"]))
        block.scalar(lambda E: run(E, self.q["s"]))
        block.gpsimd(lambda E: run(E, self.q["g"]))
        block.sync(lambda E: run(E, self.q["y"]))

    def clear(self):
        self.q = {e: [] for e in self.ENGS}


def g3(t, ginner):
    return t[:, :].rearrange("p (g c) -> p g c", c=ginner)


def build_program(n_warm=2, n_real=16, dbg=False):
    NCH = n_warm + n_real
    ROWS = NCH * CH
    T = n_real * CH
    NT = n_real
    TCK = 512 if T % 512 == 0 else T
    NTC = T // TCK
    NML = TCK // 128

    nc = bass.Bass("TRN2", target_bir_lowering=False)

    x_in = nc.dram_tensor("x", [ROWS, C], F32, kind="ExternalInput").ap()
    wk_in = nc.dram_tensor("wk", [C, C], F32, kind="ExternalInput").ap()
    wv_in = nc.dram_tensor("wv", [C, C], F32, kind="ExternalInput").ap()
    wr_in = nc.dram_tensor("wr", [C, C], F32, kind="ExternalInput").ap()
    wo_in = nc.dram_tensor("wo", [C, C], F32, kind="ExternalInput").ap()
    w1_in = nc.dram_tensor("w1", [NE, C, H], F32, kind="ExternalInput").ap()
    w2_in = nc.dram_tensor("w2", [NE, H, C], F32, kind="ExternalInput").ap()
    b2_in = nc.dram_tensor("b2", [NE, C], F32, kind="ExternalInput").ap()
    wrf_in = nc.dram_tensor("wrf", [CG, 128, NE], F32, kind="ExternalInput").ap()
    pk_in = nc.dram_tensor("pk", [128, PK_W], F32, kind="ExternalInput").ap()
    eye_in = nc.dram_tensor("eye", [128, 128], F32, kind="ExternalInput").ap()
    y_out = nc.dram_tensor("y", [T, C], F32, kind="ExternalOutput").ap()
    wscr = nc.dram_tensor("wscr", [4, C, C], BF16, kind="Internal").ap()
    dbg_out = {}
    if dbg:
        for nm, shape, dt_ in (
                ("d_xnT", [128, CG * T], BF16), ("d_xlnT", [128, CG * (CH + 1)], BF16),
                ("d_mixk", [128, CG * CH], BF16), ("d_E", [128, CG * CH], F32),
                ("d_V", [128, CG * CH], F32), ("d_stD", [128, CG], F32),
                ("d_stN", [128, CG], F32), ("d_rw", [128, 16 * NE], F32),
                ("d_G", [128, CG * CH], BF16), ("d_WkT", [128, CG * C], BF16),
                ("d_stats", [128, 64], F32), ("d_xlnat", [128, C], BF16),
                ("d_tdb", [128, CG * CH], F32), ("d_wrfb", [128, CG * NE], BF16),
                ("d_tm1m", [128, 32], F32), ("d_sc", [128, 4 * NE], F32)):
            dbg_out[nm] = nc.dram_tensor(nm, shape, dt_, kind="ExternalOutput").ap()

    import contextlib
    stack = contextlib.ExitStack()
    with stack:
        em = Emit(nc)
        for s in ("sx", "sxb", "sc", "sw", "sv", "ss", "st", "so", "sg"):
            em.new_sem(s, stack)

        A = nc.alloc_sbuf_tensor
        acc = A("acc", [128, NT * C], F32)
        xnT = A("xnT", [128, CG * T], BF16)
        pk = A("pkc", [128, PK_W], F32)
        eyeb = A("eyeb", [128, 128], BF16)
        tdb = A("tdb", [128, CG * CH], F32)
        rw_all = A("rw_all", [128, NT * NE], F32)
        rwT = A("rwT", [8, NT * 128], BF16)
        b2bf = A("b2bf", [8, C], BF16)
        wrfb = A("wrfb", [128, CG * NE], BF16)
        scr8 = A("scr8", [128, 256], F32)
        tm1m = A("tm1m", [128, 32], F32)
        trsh = A("trsh", [128, 8], F32)
        rwb = A("rwb", [128, 4 * NE], BF16)
        sc_f = A("sc_f", [128, 4 * NE], F32)

        tds_c = lambda g: pk[:, PK_TDS + g:PK_TDS + g + 1]
        tm_c = {"k": (lambda g: pk[:, PK_TMK + g:PK_TMK + g + 1]),
                "v": (lambda g: pk[:, PK_TMV + g:PK_TMV + g + 1]),
                "r": (lambda g: pk[:, PK_TMR + g:PK_TMR + g + 1])}
        g_c = lambda g: pk[:, PK_G + g:PK_G + g + 1]
        b_c = lambda g: pk[:, PK_B + g:PK_B + g + 1]
        brt = pk[:, PK_BR:PK_BR + NE]
        iota8 = pk[:, PK_IOTA:PK_IOTA + NE]
        mask_c = pk[:, PK_MASK:PK_MASK + 1]
        ones128 = pk[:, PK_ONES:PK_ONES + 128]
        b1_c = lambda e, ht32: pk[:, PK_B1 + e * 32 + ht32:PK_B1 + e * 32 + ht32 + 1]
        tm_t = {"k": pk[:, PK_TMK:PK_TMK + 8], "v": pk[:, PK_TMV:PK_TMV + 8],
                "r": pk[:, PK_TMR:PK_TMR + 8]}
        PIX = {"k": 0, "v": 1, "r": 2}

        def bump(eng, sem):
            if eng == "s":
                return em.op("s", lambda E: E.activation(
                    trsh[:, 0:1], pk[:, 0:1], AF.Copy), inc=sem)
            return em.op("v", lambda E: E.tensor_copy(
                trsh[:, 4:5], pk[:, 0:1]), inc=sem)
        eps5 = pk[:, PK_EPS5:PK_EPS5 + 1]
        eps8 = pk[:, PK_EPS8:PK_EPS8 + 1]
        onec = pk[:, PK_ONE:PK_ONE + 1]

        import contextlib as _ctxlib
        attn_stack = _ctxlib.ExitStack()
        AE = lambda nm, shp, dt_: attn_stack.enter_context(
            nc.sbuf_tensor(nm, shp, dt_))
        WT = {}
        for w in ("k", "v", "r", "o"):
            WT[w] = AE(f"W{w}T", [128, CG * C], BF16)
        xlnTb = [AE(f"xlnT{i}", [128, CG * (CH + 1)], BF16) for i in range(2)]
        mix = {"k": AE("mixk", [128, CG * CH], BF16),
               "v": AE("mixv", [128, CG * CH], BF16)}
        mix["r"] = mix["k"]
        Gbuf = AE("Gbuf", [128, CG * CH], BF16)
        xlnat = AE("xlnat", [128, C], BF16)
        xnat = AE("xnat", [128, C], BF16)
        E2 = [AE(f"Ebuf{i}", [128, CG * CH], F32) for i in range(2)]
        V2 = [AE(f"Vbuf{i}", [128, CG * CH], F32) for i in range(2)]
        dead = AE("dead", [128, 512], F32)
        stats = AE("stats", [128, 64], F32)
        stD = AE("stD", [128, CG], F32)
        stN = AE("stN", [128, CG], F32)

        pm = [stack.enter_context(nc.psum_tensor(f"pm{i}", [128, 512], F32))
              for i in range(8)]
        ptrA = pm[0][:, :].bitcast(BF16)
        ptrB = pm[1][:, :].bitcast(BF16)
        pbf7 = pm[7][:, :].bitcast(BF16)

        G_ = Gbuf

        em.op("y", lambda E: E.dma_start(out=pk[:, :], in_=pk_in), inc="sc", k=16)
        sx_x = {}
        msF = {}

        em.op("g", lambda E: E.dma_start(out=eyeb[:, :], in_=eye_in), inc="sw", k=16)
        em.op("g", lambda E: E.dma_start(
            out=g3(wrfb, NE)[:, :, :],
            in_=wrf_in.rearrange("g p e -> p g e")), inc="sw", k=16)
        em.op("g", lambda E: E.dma_start(out=b2bf[:, :], in_=b2_in), inc="sw", k=16)
        sw_base = em.cnt["sw"]

        win = {"k": wk_in, "v": wv_in, "r": wr_in, "o": wo_in}
        ms_wcast = {}
        ms_wts_done = {}
        for wi, wname in enumerate(("k", "v", "r", "o")):
            ms_wcast[wname] = em.op(
                "g", lambda E, wi=wi, wname=wname: E.dma_start(
                    out=wscr[wi], in_=win[wname]), inc="sw", k=16)
        for wi, wname in enumerate(("k", "v", "r", "o")):
            em.wait("s", "sw", ms_wcast[wname])
            for cg in range(CG):
                em.op("s", lambda E, wi=wi, wname=wname, cg=cg:
                      E.dma_start(out=g3(WT[wname], C)[:, cg, :],
                                  in_=wscr[wi][:, cg * 128:(cg + 1) * 128],
                                  transpose=True), inc="sw", k=16)
            ms_wts_done[wname] = em.cnt["sw"]

        em.wait("s", "sc", 16)
        em.op("s", lambda E: E.activation(
            tm1m[:, 24:32], pk[:, PK_TDS:PK_TDS + 8], AF.Exp))
        ms_etd = bump("s", "ss")
        em.wait("v", "sc", 16)
        em.wait("v", "ss", ms_etd)
        for g in range(CG):
            em.op("v", lambda E, g=g: E.tensor_scalar(
                tdb[:, g * CH:(g + 1) * CH], ones128[:, 0:CH],
                tm1m[:, 24 + g:24 + g + 1], None, ALU.mult))
        em.op("v", lambda E: E.tensor_scalar(
            stD[:, :], iota8, 0.0, None, ALU.mult))
        em.op("v", lambda E: E.tensor_scalar(
            stN[:, :], iota8, 0.0, None, ALU.mult))
        for pi, p in enumerate(("k", "v", "r")):
            em.op("v", lambda E, pi=pi, p=p: E.tensor_scalar(
                tm1m[:, pi * 8:pi * 8 + 8], tm_t[p], -1.0, 1.0,
                ALU.mult, ALU.add))

        STAT0 = lambda par, i: stats[:, (par % 4) * 16 + i:(par % 4) * 16 + i + 1]
        msA = {}; msG = {}; msH = {}; msI = {}
        msJk = {}; msJv = {}; msK = {}; msL = {}; msM = {}; msN = {}; msHs = {}
        msO = {}; msP = {}; msQ = {}; msR = {}; msV = {}
        bankfree23 = {}
        msXD = {}

        def emitA(c):
            real = c >= n_warm
            m = c - n_warm
            xin = acc[:, (m if real else n_real - n_warm + c) * C:
                      (m if real else n_real - n_warm + c) * C + C]
            STAT = lambda i: STAT0(c % 4, i)

            if real and c - n_real >= 0:
                em.wait("y", "sv", msF[c - n_real])
            xsem = "sx" if c % 2 == 0 else "sxb"
            em.op("y", lambda E, xr=x_in[c * CH:(c + 1) * CH, :], xd=xin:
                  E.dma_start(out=xd, in_=xr), inc=xsem, k=16)
            msXD[c] = (xsem, em.cnt[xsem])

            em.wait("s", msXD[c][0], msXD[c][1])
            if c >= 4:
                em.wait("s", "sv", msF[c - 4])
            if c >= 1:
                em.wait("s", "st", msG[c - 1])
            em.op("s", lambda E, xd=xin: E.activation(
                xnat[:, :], xd, AF.Copy, accum_out=STAT(0)))
            em.op("s", lambda E, xd=xin: E.activation(
                xnat[:, :], xd, AF.Square, accum_out=STAT(1)))
            em.op("s", lambda E: E.mul(STAT(2), STAT(0), -1.0 / C))
            em.op("s", lambda E: E.activation(STAT(4), STAT(2), AF.Square))
            em.op("s", lambda E: E.mul(STAT(5), STAT(4), -1.0))
            em.op("s", lambda E: E.mul(STAT(3), STAT(1), 1.0 / C))
            em.op("s", lambda E: E.add(STAT(6), STAT(3), STAT(5)))
            em.op("s", lambda E: E.activation(STAT(7), STAT(6), AF.Ln, bias=eps5))
            em.op("s", lambda E: E.activation(
                STAT(8), STAT(7), AF.Exp, scale=-0.5))
            em.op("s", lambda E: E.activation(STAT(10), STAT(6), AF.Ln))
            em.op("s", lambda E: E.mul(STAT(9), STAT(7), -1.0))
            em.op("s", lambda E: E.mul(STAT(14), STAT(7), -0.5))
            em.op("s", lambda E: E.activation(STAT(12), STAT(10), AF.Exp,
                                              bias=STAT(9)))
            em.op("s", lambda E: E.activation(STAT(13), STAT(12), AF.Ln, bias=eps5))
            em.op("s", lambda E: E.activation(
                STAT(15), STAT(13), AF.Exp, scale=-0.5, bias=STAT(14)))
            msA[c] = bump("s", "ss")

            em.wait("v", "ss", msA[c])
            if c >= 1:
                em.wait("v", "st", msG[c - 1])
            fn = lambda E, xd=xin: E.tensor_scalar(
                xlnat[:, :], xd, STAT(2), STAT(15), ALU.add, ALU.mult)
            if real:
                em.op("v", fn)
                msF[c] = em.op("v", lambda E, xd=xin: E.tensor_scalar(
                    xnat[:, :], xd, STAT(2), STAT(8), ALU.add, ALU.mult),
                    inc="sv")
            else:
                msF[c] = em.op("v", fn, inc="sv")

            em.wait("t", "sv", msF[c])
            if c >= 1:
                em.wait("t", "sv", msH[c - 1])
                if (c - 1) in msHs:
                    em.wait("t", "ss", msHs[c - 1])
            if real:
                for cg in range(CG):
                    em.op("t", lambda E, cg=cg: E.transpose(
                        ptrA[:, cg * 128:(cg + 1) * 128],
                        xnat[:, cg * 128:(cg + 1) * 128], eyeb[:, :]))
            for cg in range(CG):
                fn = lambda E, cg=cg: E.transpose(
                    ptrB[:, cg * 128:(cg + 1) * 128],
                    xlnat[:, cg * 128:(cg + 1) * 128], eyeb[:, :])
                if cg == CG - 1:
                    msG[c] = em.op("t", fn, inc="st")
                else:
                    em.op("t", fn)

            em.wait("v", "st", msG[c])
            xlnT3 = g3(xlnTb[c % 2], CH + 1)
            if c == 0:
                em.op("v", lambda E: E.tensor_scalar(
                    xlnT3[:, :, 0], iota8[:, :], 0.0, None, ALU.mult))
            else:
                prev3 = g3(xlnTb[(c - 1) % 2], CH + 1)
                em.op("v", lambda E, p3=prev3: E.tensor_copy(
                    xlnT3[:, :, 0], p3[:, :, CH]))
            if real:
                em.wait("s", "st", msG[c])
                em.op("s", lambda E, m=m: E.copy(
                    g3(xnT, T)[:, :, m * CH:(m + 1) * CH],
                    ptrA.rearrange("p (g c) -> p g c", c=128)))
                msHs[c] = em.inc_last("s", "ss")
            for cg in range(CG):
                fn = lambda E, cg=cg: E.tensor_scalar(
                    xlnT3[:, cg, 1:CH + 1], ptrB[:, cg * 128:(cg + 1) * 128],
                    g_c(cg), b_c(cg), ALU.mult, ALU.add)
                if cg == CG - 1:
                    msH[c] = em.op("v", fn, inc="sv")
                else:
                    em.op("v", fn)

        def emitB(c):
            real = c >= n_warm
            m = c - n_warm
            E_ = E2[c % 2]
            V_ = V2[c % 2]
            xlnT3 = g3(xlnTb[c % 2], CH + 1)

            def emit_mix(p):
                buf = mix[p]
                if p == "k":
                    pass
                else:
                    em.op("v", lambda E, buf=buf: E.tensor_tensor(
                        g3(buf, CH)[:, :, :], xlnT3[:, :, 1:CH + 1],
                        xlnT3[:, :, 0:CH], ALU.subtract))
                src = mix["v"] if p == "k" else buf
                for cg in range(CG):
                    em.op("v", lambda E, p=p, cg=cg, buf=buf, src=src:
                          E.tensor_scalar(
                              buf[:, cg * CH:(cg + 1) * CH],
                              src[:, cg * CH:(cg + 1) * CH],
                              tm_c[p](cg), None, ALU.mult))
                em.op("v", lambda E, buf=buf: E.tensor_tensor(
                    g3(buf, CH)[:, :, :], g3(buf, CH)[:, :, :],
                    xlnT3[:, :, 0:CH], ALU.add))

            def emit_patch(pi, p):
                em.op("v", lambda E, p=p, pi=pi: E.tensor_mul(
                    scr8[:, 16 + 8 * pi:24 + 8 * pi], scr8[:, 8:16], tm_t[p]))
                em.op("v", lambda E: E.tensor_copy(trsh[:, 4:5], pk[:, 0:1]))
                em.op("v", lambda E, p=p, pi=pi: E.tensor_add(
                    g3(mix[p], CH)[:, :, 0], scr8[:, 16 + 8 * pi:24 + 8 * pi],
                    scr8[:, 0:8]))

            em.wait("v", "st", msG[c])
            if c >= 1:
                if (c - 1) in msL:
                    em.wait("v", "st", msL[c - 1])
                em.wait("v", "st", msJv[c - 1])
            if c == n_warm:
                em.op("v", lambda E: E.tensor_scalar(
                    scr8[:, 0:8], xlnT3[:, :, 0], mask_c, None, ALU.mult))
                em.op("v", lambda E: E.tensor_copy(trsh[:, 4:5], pk[:, 0:1]))
                em.op("v", lambda E: E.tensor_sub(
                    scr8[:, 8:16], xlnT3[:, :, 1], scr8[:, 0:8]))
                em.op("v", lambda E: E.tensor_copy(trsh[:, 4:5], pk[:, 0:1]))
            em.op("v", lambda E: E.tensor_tensor(
                g3(mix["v"], CH)[:, :, :], xlnT3[:, :, 1:CH + 1],
                xlnT3[:, :, 0:CH], ALU.subtract))
            emit_mix("k")
            emit_mix("v")
            if c == n_warm:
                emit_patch(0, "k")
                emit_patch(1, "v")
            msI[c] = em.inc_last("v", "sv")

            em.wait("t", "sv", msI[c])
            if c >= 1:
                em.wait("t", "ss", bankfree23[c - 1])
                em.wait("t", "ss", msV[c - 1])
            if (c - 2) in msQ:
                em.wait("t", "sv", msQ[c - 2])
            if c == 0:
                em.wait("t", "sw", ms_wts_done["v"])
            for (p, bofs, msd) in (("k", 2, msJk), ("v", 4, msJv)):
                for coT in range(CG):
                    bank = pm[bofs + coT // 4]
                    cofs = (coT % 4) * 128
                    for cg in range(CG):
                        fn = (lambda E, p=p, coT=coT, cg=cg, b=bank, co=cofs:
                              E.matmul(b[:, co:co + 128],
                                       g3(WT[p], C)[:, cg, coT * 128:(coT + 1) * 128],
                                       g3(mix[p], CH)[:, cg, :],
                                       start=(cg == 0), stop=(cg == CG - 1)))
                        if coT == CG - 1 and cg == CG - 1:
                            msd[c] = em.op("t", fn, inc="st")
                        else:
                            em.op("t", fn)

            em.wait("s", "st", msJk[c])
            if c >= 2:
                em.wait("s", "sg", msP[c - 2])
            em.op("s", lambda E: E.activation(E_[:, 0:512], pm[2][:, :], AF.Exp))
            em.op("s", lambda E: E.activation(E_[:, 512:1024], pm[3][:, :], AF.Exp))
            em.wait("s", "st", msJv[c])
            em.op("s", lambda E: E.copy(V_[:, 0:512], pm[4][:, :]))
            msK[c] = em.op("s", lambda E: E.copy(
                V_[:, 512:1024], pm[5][:, :]), inc="ss")
            msV[c] = msK[c]
            bankfree23[c] = msK[c]

            if real:
                em.wait("v", "st", msJk[c])
                emit_mix("r")
                if c == n_warm:
                    emit_patch(2, "r")
                msIr = em.inc_last("v", "sv")

            if real:
                em.wait("t", "sv", msIr)
                if (c - 1) in msO:
                    em.wait("t", "ss", msO[c - 1])
                if c == n_warm:
                    em.wait("t", "sw", ms_wts_done["r"])
                for coT in range(CG):
                    bank = pm[6 + coT // 4]
                    cofs = (coT % 4) * 128
                    for cg in range(CG):
                        fn = (lambda E, coT=coT, cg=cg, b=bank, co=cofs:
                              E.matmul(b[:, co:co + 128],
                                       g3(WT["r"], C)[:, cg, coT * 128:(coT + 1) * 128],
                                       g3(mix["r"], CH)[:, cg, :],
                                       start=(cg == 0), stop=(cg == CG - 1)))
                        if coT == CG - 1 and cg == CG - 1:
                            msL[c] = em.op("t", fn, inc="st")
                        else:
                            em.op("t", fn)

            em.wait("g", "ss", msK[c])
            msKV = em.op("g", lambda E: E.tensor_mul(
                V_[:, :], E_[:, :], V_[:, :]), inc="sg")
            em.wait("v", "ss", msK[c])
            em.wait("v", "sg", msKV)
            for cg in range(CG):
                sl = slice(cg * CH, (cg + 1) * CH)
                em.op("v", lambda E, sl=sl, cg=cg: E.tensor_tensor_scan(
                    E_[:, sl], tdb[:, sl], E_[:, sl], stD[:, cg:cg + 1],
                    ALU.mult, ALU.add))
            for cg in range(CG):
                sl = slice(cg * CH, (cg + 1) * CH)
                em.op("v", lambda E, sl=sl, cg=cg: E.tensor_tensor_scan(
                    V_[:, sl], tdb[:, sl], V_[:, sl], stN[:, cg:cg + 1],
                    ALU.mult, ALU.add))
            em.op("v", lambda E: E.tensor_copy(stD[:, :], g3(E_, CH)[:, :, CH - 1]))
            em.op("v", lambda E: E.tensor_copy(trsh[:, 4:5], pk[:, 0:1]))
            msM[c] = em.op("v", lambda E: E.tensor_copy(
                stN[:, :], g3(V_, CH)[:, :, CH - 1]), inc="sv")


        def emitBback(c):
            real = c >= n_warm
            E_ = E2[c % 2]
            V_ = V2[c % 2]

            em.wait("s", "sv", msM[c])
            for hh in range(2):
                em.op("s", lambda E, hh=hh: E.activation(
                    dead[:, :], E_[:, hh * 512:(hh + 1) * 512], AF.Ln,
                    bias=eps8))
                fn = lambda E, hh=hh: E.activation(
                    E_[:, hh * 512:(hh + 1) * 512], dead[:, :], AF.Exp,
                    scale=-1.0)
                if hh == 1:
                    msN[c] = em.op("s", fn, inc="ss")
                else:
                    em.op("s", fn)

            if real:
                em.wait("s", "st", msL[c])
                if (c - 1) in msR:
                    em.wait("s", "st", msR[c - 1])
                for bi, b in enumerate((6, 7)):
                    em.op("s", lambda E, b=b: E.activation(
                        dead[:, 0:512], pm[b][:, :], AF.Exp, scale=-1.0))
                    em.op("s", lambda E: E.activation(
                        dead[:, 0:512], dead[:, 0:512], AF.Ln, bias=onec))
                    fn = lambda E, bi=bi: E.activation(
                        G_[:, bi * 512:(bi + 1) * 512], dead[:, 0:512],
                        AF.Exp, scale=-1.0)
                    if bi == 1:
                        msO[c] = em.op("s", fn, inc="ss")
                    else:
                        em.op("s", fn)

            em.wait("g", "ss", msN[c])
            em.wait("g", "sv", msM[c])
            fn = lambda E: E.tensor_mul(V_[:, :], V_[:, :], E_[:, :])
            if real:
                em.op("g", fn)
                em.wait("g", "ss", msO[c])
                msP[c] = em.op("g", lambda E: E.tensor_mul(
                    G_[:, :], V_[:, :], G_[:, :]), inc="sg")
            else:
                msP[c] = em.op("g", fn, inc="sg")

        def emitWo(c, free_ss):
            if c < n_warm:
                return
            m = c - n_warm
            em.wait("t", "sg", msP[c])
            em.wait("t", "ss", free_ss)
            if (c - 1) in msQ:
                em.wait("t", "sv", msQ[c - 1])
            if c == n_warm:
                em.wait("t", "sw", ms_wts_done["o"])
            for half in range(2):
                for cg in range(CG):
                    fn = (lambda E, half=half, cg=cg:
                          E.matmul(pm[2 + half][:, :],
                                   g3(G_, CH)[:, cg, :],
                                   g3(WT["o"], C)[:, cg,
                                                  half * 512:(half + 1) * 512],
                                   start=(cg == 0), stop=(cg == CG - 1)))
                    if half == 1 and cg == CG - 1:
                        msR[c] = em.op("t", fn, inc="st")
                    else:
                        em.op("t", fn)

        def emitAcc(c):
            if c < n_warm:
                return
            m = c - n_warm
            em.wait("v", "st", msR[c])
            em.op("v", lambda E, m=m: E.tensor_add(
                acc[:, m * C:m * C + 512], acc[:, m * C:m * C + 512],
                pm[2][:, :]))
            msQ[c] = em.op("v", lambda E, m=m: E.tensor_add(
                acc[:, m * C + 512:m * C + 1024],
                acc[:, m * C + 512:m * C + 1024], pm[3][:, :]), inc="sv")

        emitA(0)
        for c in range(NCH):
            if c + 1 < NCH:
                emitA(c + 1)
            if c >= 1:
                emitBback(c - 1)
            emitB(c)
            if c >= 1:
                emitWo(c - 1, msK[c])
                emitAcc(c - 1)
        emitBback(NCH - 1)
        emitWo(NCH - 1, msK[NCH - 1])
        emitAcc(NCH - 1)

        last = NCH - 1
        attn_v_done = em.cnt["sv"]
        attn_s_done = em.cnt["ss"]

        ms_b2 = {}

        GL = 4 if NT % 4 == 0 else (2 if NT % 2 == 0 else 1)
        NG = NT // GL
        ms_grp_read = {}
        em.wait("t", "sv", msQ[last])
        em.wait("t", "ss", attn_s_done)
        LC = lambda l, a, b: scr8[:, l * 64 + a:l * 64 + b]
        SCL = lambda l: sc_f[:, l * NE:(l + 1) * NE]
        for grp in range(NG):
            mis = [grp * GL + l for l in range(GL)]
            if grp >= 1:
                em.wait("t", "sv", ms_grp_read[grp - 1])
            for l, mi in enumerate(mis):
                for cg in range(CG):
                    em.op("t", lambda E, l=l, mi=mi, cg=cg: E.matmul(
                        pm[6][:, l * NE:l * NE + NE],
                        g3(xnT, T)[:, cg, mi * CH:mi * CH + 128],
                        g3(wrfb, NE)[:, cg, :],
                        start=(cg == 0), stop=(cg == CG - 1)))
            ms_scg = em.inc_last("t", "st")
            em.wait("v", "st", ms_scg)

            def lanes(fn):
                for l in range(GL):
                    em.op("v", lambda E, l=l: fn(E, l))
            lanes(lambda E, l: E.tensor_add(
                SCL(l), pm[6][:, l * NE:l * NE + NE], brt))
            ms_grp_read[grp] = bump("v", "sv")
            lanes(lambda E, l: E.tensor_scalar(
                SCL(l), SCL(l), 0.0, None, ALU.max))
            lanes(lambda E, l: E.reduce_max(LC(l, 0, 1), SCL(l), axis=AX.X))
            lanes(lambda E, l: E.tensor_scalar(
                LC(l, 8, 16), SCL(l), LC(l, 0, 1), None, ALU.is_equal))
            lanes(lambda E, l: E.tensor_scalar(
                LC(l, 8, 16), LC(l, 8, 16), -1e9, 1e9, ALU.mult, ALU.add))
            lanes(lambda E, l: E.tensor_add(LC(l, 8, 16), LC(l, 8, 16), iota8))
            lanes(lambda E, l: E.tensor_reduce(
                LC(l, 1, 2), LC(l, 8, 16), axis=AX.X, op=ALU.min))
            lanes(lambda E, l: E.tensor_scalar(
                LC(l, 16, 24), iota8, LC(l, 1, 2), None, ALU.is_equal))
            lanes(lambda E, l: E.scalar_tensor_tensor(
                LC(l, 24, 32), LC(l, 16, 24), -1e9, SCL(l), ALU.mult, ALU.add))
            lanes(lambda E, l: E.reduce_max(LC(l, 2, 3), LC(l, 24, 32), axis=AX.X))
            lanes(lambda E, l: E.tensor_scalar(
                LC(l, 32, 40), LC(l, 24, 32), LC(l, 2, 3), None, ALU.is_equal))
            lanes(lambda E, l: E.tensor_scalar(
                LC(l, 32, 40), LC(l, 32, 40), -1e9, 1e9, ALU.mult, ALU.add))
            lanes(lambda E, l: E.tensor_add(LC(l, 32, 40), LC(l, 32, 40), iota8))
            lanes(lambda E, l: E.tensor_reduce(
                LC(l, 3, 4), LC(l, 32, 40), axis=AX.X, op=ALU.min))
            lanes(lambda E, l: E.tensor_scalar(
                LC(l, 40, 48), iota8, LC(l, 3, 4), None, ALU.is_equal))
            lanes(lambda E, l: E.tensor_sub(LC(l, 4, 5), LC(l, 2, 3), LC(l, 0, 1)))
            ms_d = bump("v", "sv")
            em.wait("s", "sv", ms_d)
            for l in range(GL):
                em.op("s", lambda E, l=l: E.activation(
                    LC(l, 5, 6), LC(l, 4, 5), AF.Exp))
            ms_e2 = bump("s", "ss")
            em.wait("v", "ss", ms_e2)
            lanes(lambda E, l: E.tensor_scalar(
                LC(l, 6, 7), LC(l, 5, 6), 1.0, None, ALU.add))
            lanes(lambda E, l: E.reciprocal(LC(l, 6, 7), LC(l, 6, 7)))
            lanes(lambda E, l: E.tensor_mul(LC(l, 7, 8), LC(l, 5, 6), LC(l, 6, 7)))
            lanes(lambda E, l: E.tensor_scalar(
                LC(l, 48, 56), LC(l, 16, 24), LC(l, 6, 7), None, ALU.mult))
            lanes(lambda E, l: E.tensor_scalar(
                LC(l, 56, 64), LC(l, 40, 48), LC(l, 7, 8), None, ALU.mult))
            for l, mi in enumerate(mis):
                em.op("v", lambda E, l=l, mi=mi: E.tensor_add(
                    rw_all[:, mi * NE:(mi + 1) * NE],
                    LC(l, 48, 56), LC(l, 56, 64)))
            for l, mi in enumerate(mis):
                em.op("v", lambda E, l=l, mi=mi: E.tensor_copy(
                    rwb[:, l * NE:(l + 1) * NE],
                    rw_all[:, mi * NE:(mi + 1) * NE]))
            ms_rwbg = bump("v", "sv")
            em.wait("t", "sv", ms_rwbg)
            for l in range(GL):
                em.op("t", lambda E, l=l: E.transpose(
                    pbf7[0:8, l * 128:(l + 1) * 128],
                    rwb[:, l * NE:(l + 1) * NE], eyeb[:, :]))
            ms_rwtg = em.inc_last("t", "st")
            em.wait("v", "st", ms_rwtg)
            for l, mi in enumerate(mis):
                em.op("v", lambda E, l=l, mi=mi: E.tensor_copy(
                    rwT[:, mi * 128:(mi + 1) * 128],
                    pbf7[0:8, l * 128:(l + 1) * 128]))
            ms_cpg = bump("v", "sv")
            em.wait("t", "sv", ms_cpg)
            for l, mi in enumerate(mis):
                for half in range(2):
                    msb = em.op("t", lambda E, mi=mi, half=half: E.matmul(
                        pm[4 + half][:, :], rwT[:, mi * 128:(mi + 1) * 128],
                        b2bf[:, half * 512:(half + 1) * 512],
                        start=True, stop=True), inc="st")
                    em.wait("v", "st", msb)
                    ms_b2[(mi, half)] = em.op(
                        "v", lambda E, mi=mi, half=half: E.tensor_add(
                            acc[:, mi * C + half * 512:mi * C + half * 512 + 512],
                            acc[:, mi * C + half * 512:mi * C + half * 512 + 512],
                            pm[4 + half][:, :]), inc="sv")
                    em.wait("t", "sv", ms_b2[(mi, half)])

        router_v_done = em.cnt["sv"]
        router_s_done = em.cnt["ss"]
        router_t_done = em.cnt["st"]

        with nc.Block() as blk1:
            em.play(blk1)
        em.clear()
        if not dbg:
            attn_stack.close()
        else:
            stack.enter_context(attn_stack)
        w1q = [stack.enter_context(nc.sbuf_tensor(f"w1q{i}", [128, CG * HQ],
                                                  BF16)) for i in range(2)]
        w2q = [stack.enter_context(nc.sbuf_tensor(f"w2q{i}", [128, CG * HQ],
                                                  BF16)) for i in range(2)]
        hT = [stack.enter_context(nc.sbuf_tensor(f"hT{i}", [128, CG * TCK],
                                                 BF16)) for i in range(2)]

        ms_wffn = {}
        ms_ffn_t = {}

        def emit_ffn_dma(i):
            e, q = divmod(i, NQ)
            if i >= 2:
                em.wait("g", "st", ms_ffn_t[i - 2])
            elif i == 0:
                em.wait("g", "st", msL[NCH - 1])
                em.wait("g", "sc", 16)
            else:
                em.wait("g", "st", msR[NCH - 1])
                em.wait("g", "sc", 16)
            src1 = w1_in[e, :, q * HQ:(q + 1) * HQ] \
                .rearrange("(g p) h -> p g h", p=128)
            src2 = w2_in[e, q * HQ:(q + 1) * HQ, :] \
                .rearrange("(g p) d -> p g d", p=128)
            em.op("g", lambda E, s=src1, i=i: E.dma_start(
                out=g3(w1q[i % 2], HQ)[:, :, :], in_=s), inc="sw", k=16)
            ms_wffn[i] = em.op("g", lambda E, s=src2, i=i: E.dma_start(
                out=g3(w2q[i % 2], HQ)[:, :, :], in_=s), inc="sw", k=16)

        emit_ffn_dma(0)
        emit_ffn_dma(1)

        em.wait("t", "sv", router_v_done)
        em.wait("t", "ss", attn_s_done)
        bank_free_h = {}
        bank_free_y = {}
        hT_reader = {}
        h_done = {}
        NJ = NE * NQ * NTC

        def emit_h(k):
            i, n = divmod(k, NTC)
            e, q = divmod(i, NQ)
            if n == 0:
                em.wait("t", "sw", ms_wffn[i])
            for ht in range(CG):
                b = 2 + (ht % 2)
                if b in bank_free_h:
                    em.wait("t", "ss", bank_free_h[b])
                for cg in range(CG):
                    fn = (lambda E, i=i, n=n, ht=ht, cg=cg, b=b:
                          E.matmul(pm[b][:, 0:TCK],
                                   g3(w1q[i % 2], HQ)[:, cg,
                                                      ht * 128:(ht + 1) * 128],
                                   g3(xnT, T)[:, cg, n * TCK:(n + 1) * TCK],
                                   start=(cg == 0), stop=(cg == CG - 1)))
                    em.op("t", fn)
                ms_h = em.inc_last("t", "st")
                em.wait("s", "st", ms_h)
                if ht == 0 and (k % 2) in hT_reader:
                    em.wait("s", "st", hT_reader[k % 2])
                bank_free_h[b] = em.op(
                    "s", lambda E, e=e, q=q, k=k, ht=ht, b=b: E.activation(
                        g3(hT[k % 2], TCK)[:, ht, :], pm[b][:, 0:TCK],
                        AF.Relu, bias=b1_c(e, q * 8 + ht)), inc="ss")
            h_done[k] = em.cnt["ss"]

        def emit_y(k):
            i, n = divmod(k, NTC)
            e, q = divmod(i, NQ)
            em.wait("t", "ss", h_done[k])
            for ml in range(NML):
                mi = n * NML + ml
                for half in range(2):
                    b = 4 + half
                    if b in bank_free_y:
                        em.wait("t", "sv", bank_free_y[b])
                    for hg in range(CG):
                        fn = (lambda E, i=i, k=k, ml=ml, half=half, hg=hg, b=b:
                              E.matmul(pm[b][:, :],
                                       g3(hT[k % 2], TCK)[:, hg,
                                                          ml * 128:(ml + 1) * 128],
                                       g3(w2q[i % 2], HQ)[:, hg,
                                                          half * 512:(half + 1) * 512],
                                       start=(hg == 0), stop=(hg == CG - 1)))
                        em.op("t", fn)
                    ms_y = em.inc_last("t", "st")
                    hT_reader[k % 2] = ms_y
                    em.wait("v", "st", ms_y)
                    bank_free_y[b] = em.op(
                        "v", lambda E, e=e, mi=mi, half=half, b=b:
                        E.scalar_tensor_tensor(
                            acc[:, mi * C + half * 512:mi * C + half * 512 + 512],
                            pm[b][:, :],
                            rw_all[:, mi * NE + e:mi * NE + e + 1],
                            acc[:, mi * C + half * 512:mi * C + half * 512 + 512],
                            ALU.mult, ALU.add), inc="sv")

        emit_h(0)
        for k in range(NJ):
            if k + 1 < NJ:
                emit_h(k + 1)
            emit_y(k)
            i, n = divmod(k, NTC)
            if n == NTC - 1:
                ms_ffn_t[i] = em.cnt["st"]
                if i + 2 < NE * NQ:
                    emit_ffn_dma(i + 2)

        em.wait("y", "sv", em.cnt["sv"])
        em.wait("y", "ss", em.cnt["ss"])
        em.wait("y", "st", em.cnt["st"])
        if dbg:
            ndb = 0
            for nm, buf in (("d_xnT", xnT), ("d_xlnT", xlnTb[(NCH - 1) % 2]), ("d_mixk", mix["k"]),
                            ("d_E", E2[(NCH - 1) % 2]), ("d_V", V2[(NCH - 1) % 2]), ("d_stD", stD),
                            ("d_stN", stN), ("d_G", G_), ("d_WkT", WT["k"]),
                            ("d_stats", stats), ("d_xlnat", xlnat),
                            ("d_tdb", tdb), ("d_wrfb", wrfb),
                            ("d_tm1m", tm1m), ("d_sc", sc_f)):
                em.op("y", lambda E, o=dbg_out[nm], b=buf: E.dma_start(
                    out=o, in_=b[:, :]), inc="so", k=16)
                ndb += 1
            em.op("y", lambda E, o=dbg_out["d_rw"], b=rw_all: E.dma_start(
                out=o[:, 0:NT * NE], in_=b[:, :]), inc="so", k=16)
            ndb += 1
            em.wait("y", "so", 16 * ndb)
        em.op("y", lambda E: E.dma_start(
            out=y_out.rearrange("(m p) d -> p m d", p=128),
            in_=acc[:, :].rearrange("p (m d) -> p m d", d=C)), inc="so", k=16)
        em.wait("y", "so", 16)

        with nc.Block() as blk2:
            em.play(blk2)
        em.clear()

    return nc


def _shard_inputs(inputs, n_warm=2, n_real=16):
    x = np.ascontiguousarray(inputs["L0_attn_x"], dtype=np.float32)
    B, S, _ = x.shape
    T = n_real * CH
    PRE = n_warm * CH
    ncore_per_b = S // T

    def pack_row(a):
        return np.ascontiguousarray(a, dtype=np.float32).reshape(CG, 128).T

    pk = np.zeros((128, PK_W), np.float32)
    pk[:, PK_TDS:PK_TDS + 8] = pack_row(inputs["L0_attn_time_decay"])
    pk[:, PK_TMK:PK_TMK + 8] = pack_row(inputs["L0_attn_time_mix_k"].reshape(-1))
    pk[:, PK_TMV:PK_TMV + 8] = pack_row(inputs["L0_attn_time_mix_v"].reshape(-1))
    pk[:, PK_TMR:PK_TMR + 8] = pack_row(inputs["L0_attn_time_mix_r"].reshape(-1))
    pk[:, PK_G:PK_G + 8] = pack_row(inputs["L0_attn_ln_g"])
    pk[:, PK_B:PK_B + 8] = pack_row(inputs["L0_attn_ln_b"])
    pk[:, PK_BR:PK_BR + 8] = np.broadcast_to(
        np.asarray(inputs["L0_ffn_br"], np.float32), (128, 8))
    pk[:, PK_IOTA:PK_IOTA + 8] = np.broadcast_to(
        np.arange(8, dtype=np.float32), (128, 8))
    pk[:, PK_ONES:PK_ONES + 128] = 1.0
    b1 = np.asarray(inputs["L0_ffn_b1"], np.float32)
    pk[:, PK_B1:PK_B1 + 256] = b1.reshape(8, 32, 128).transpose(2, 0, 1) \
        .reshape(128, 256)
    pk[:, PK_EPS5] = 1e-5
    pk[:, PK_EPS8] = 1e-8
    pk[:, PK_ONE] = 1.0

    common = {
        "wk": np.ascontiguousarray(inputs["L0_attn_Wk"], np.float32),
        "wv": np.ascontiguousarray(inputs["L0_attn_Wv"], np.float32),
        "wr": np.ascontiguousarray(inputs["L0_attn_Wr"], np.float32),
        "wo": np.ascontiguousarray(inputs["L0_attn_Wo"], np.float32),
        "w1": np.ascontiguousarray(inputs["L0_ffn_W1"], np.float32),
        "w2": np.ascontiguousarray(inputs["L0_ffn_W2"], np.float32),
        "b2": np.ascontiguousarray(inputs["L0_ffn_b2"], np.float32),
        "wrf": np.ascontiguousarray(inputs["L0_ffn_Wr"], np.float32)
            .reshape(CG, 128, NE),
        "eye": np.eye(128, dtype=np.float32),
    }

    in_maps = []
    for b in range(B):
        for j in range(ncore_per_b):
            t0 = j * T
            if t0 - PRE >= 0:
                xs = x[b, t0 - PRE:t0 + T]
                msk = 1.0
            else:
                xs = np.concatenate(
                    [np.zeros((PRE - t0, C), np.float32), x[b, 0:t0 + T]], axis=0)
                msk = 0.0
            pkc = pk.copy()
            pkc[:, PK_MASK] = msk
            m = dict(common)
            m["x"] = np.ascontiguousarray(xs)
            m["pk"] = pkc
            in_maps.append(m)
    return in_maps


_NC_CACHE = {}


def kernel(**inputs):
    n_warm, n_real = 2, 16
    key = (n_warm, n_real)
    if key not in _NC_CACHE:
        _NC_CACHE[key] = build_program(n_warm, n_real)
    nc = _NC_CACHE[key]
    in_maps = _shard_inputs(inputs, n_warm, n_real)
    res = run_bass_kernel_spmd(nc, in_maps, core_ids=list(range(8)))
    x = inputs["L0_attn_x"]
    B, S, _ = x.shape
    T = n_real * CH
    out = np.empty((B, S, C), np.float32)
    ci = 0
    for b in range(B):
        for j in range(S // T):
            out[b, j * T:(j + 1) * T] = res.results[ci]["y"]
            ci += 1
    return out
